# version 17
# baseline (speedup 1.0000x reference)
import numpy as np

import concourse.bass as bass
import concourse.mybir as mybir
import concourse.bass_isa as bass_isa
from concourse.bass_utils import run_bass_kernel_spmd

F32 = mybir.dt.float32
BF16 = mybir.dt.bfloat16
ALU = mybir.AluOpType
ACT = mybir.ActivationFunctionType
AXL = mybir.AxisListType
RED = bass_isa.ReduceOp

T = 2048
D = 1024
NKC = 8
NTC = 16
NPAIR = 8
ENGINES = ("tensor", "vector", "scalar", "sync", "gpsimd")


def _build(upto=99, _PH5_BIAS=True, _PH5_S=True, _PH5_ELU=True, _PH5_QKS=True, _PH5_MM=True, dump=()):
    nc = bass.Bass("TRN2", target_bir_lowering=False,
                   detect_race_conditions=False)

    x_d = nc.dram_tensor("x", [T, D], F32, kind="ExternalInput")
    out_d = nc.dram_tensor("out", [T, D], F32, kind="ExternalOutput")
    dd = {}
    for n in ["Wq", "Wk", "Wv", "Wq2", "Wk2", "Wv2", "Wo", "fk"]:
        dd[n] = nc.dram_tensor(n, [D, D], F32, kind="ExternalInput")
    for n in ["bq", "bk", "bv", "bq2", "bk2", "bv2", "bo", "gamma", "beta",
              "fbias"]:
        dd[n] = nc.dram_tensor(n, [D], F32, kind="ExternalInput")
    dd["fa"] = nc.dram_tensor("fa", [D, 16], F32, kind="ExternalInput")
    dd["fb"] = nc.dram_tensor("fb", [16, D], F32, kind="ExternalInput")

    def wr(n):
        return dd[n].rearrange("(c p) j -> p c j", p=128)

    def rowr(n):
        return dd[n].rearrange("(o j) -> o j", o=1)

    from contextlib import ExitStack
    es = ExitStack()

    A = nc.alloc_sbuf_tensor
    ident_b = A("ident_b", [128, 128], BF16)
    ident_f = A("ident_f", [128, 128], F32)
    ones_col = A("ones_col", [1, 128], BF16)
    ones_f = A("ones_f", [1, 1], F32)
    xnT = A("xnT", [128, NKC, T], BF16)
    Wo_sb = A("Wo_sb", [128, NKC, D], BF16)
    Weff = A("Weff", [128, NKC, D], BF16)
    Wpv = A("Wpv", [128, NKC, D], BF16)
    s1_sb = A("s1_sb", [128, NTC], F32)
    ssq1_sb = A("ssq1_sb", [128, NTC], F32)
    nm1_sb = A("nm1_sb", [128, NTC], F32)
    w1_sb = A("w1_sb", [128, NTC], F32)
    rs1_sb = A("rs1_sb", [128, NTC], F32)
    nmrs1_sb = A("nmrs1_sb", [128, NTC], F32)
    rs0_sb = A("rs0_sb", [128, NTC], F32)
    s2_sb = A("s2_sb", [128, NTC], F32)
    ssq2_sb = A("ssq2_sb", [128, NTC], F32)
    nm2_sb = A("nm2_sb", [128, NTC], F32)
    w2_sb = A("w2_sb", [128, NTC], F32)
    rs2_sb = A("rs2_sb", [128, NTC], F32)
    nmrs2_sb = A("nmrs2_sb", [128, NTC], F32)
    rs02_sb = A("rs02_sb", [128, NTC], F32)
    tmpA = A("tmpA", [128, 1], F32)
    tmpB = A("tmpB", [128, 1], F32)
    tmpC = A("tmpC", [128, 1], F32)
    tmpS = A("tmpS", [128, 1], F32)
    qs_sb = A("qs_sb", [128, 8], F32)
    qs4_sb = A("qs4_sb", [128, 4], F32)
    qks_sb = A("qks_sb", [128, NTC * 16], F32)
    rqA_sb = A("rqA_sb", [128, NTC * 16], F32)
    rqB_sb = A("rqB_sb", [128, NTC * 16], F32)
    Sbd = A("Sbd", [128, NPAIR, 128], BF16)
    bqp_row = A("bqp_row", [1, D], BF16)
    bkp_row = A("bkp_row", [1, D], BF16)
    bo_row = A("bo_row", [1, D], BF16)
    cfb_row = A("cfb_row", [1, D], BF16)
    gamma_row = A("gamma_row", [1, D], F32)
    bvp_col = A("bvp_col", [128, 8], F32)
    bcol_st = A("bcol_st", [128, 8], BF16)
    fa_sb = A("fa_sb", [128, NKC, 16], BF16)
    faT = A("faT", [16, D], BF16)
    fb_sb = A("fb_sb", [16, D], BF16)
    selp = A("selp", [16, NPAIR, 2, 64], BF16)
    ones_row_f = A("ones_row_f", [1, 128], F32)
    qrow = A("qrow", [1, 128], F32)
    bar_s = A("bar_s", [1, 1], F32)
    bar_g = A("bar_g", [1, 1], F32)

    ARENA = 98304
    arena = A("arena", [128, ARENA], mybir.dt.uint8)
    arena_addr = nc.lookup_mloc(arena).addr

    def place(name, shape, dtype, off):
        h = nc.virtual_sbuf_tensor(name, list(shape), dtype)
        nc._tensor(h.name, h.shape, h.dtype, type="SB")
        mloc = nc.lookup_mloc(h)
        mloc.allocated = True
        mloc.addr = arena_addr + off
        return h

    xt = [place(f"xt{i}", [128, D], F32, i * 4096) for i in range(8)]
    xn_tm = place("xn_tm", [128, NTC, D], BF16, 32768)
    fbias_row = place("fbias_row", [1, D], F32, 65536)
    beta_row = place("beta_row", [1, D], F32, 69632)
    cfb_f32 = place("cfb_f32", [1, D], F32, 73728)
    garb_s = place("garb_s", [128, D], F32, 77824)
    garb_v = place("garb_v", [128, D], F32, 81920)
    fkh = [place(f"fkh{h}", [128, 4, D], F32, h * 16384) for h in range(2)]
    F_w = place("F_w", [128, NKC, D], BF16, 32768)
    E_w = place("E_w", [128, NKC, D], BF16, 49152)
    WT_w = place("WT_w", [128, NKC, D], BF16, 65536)
    stageA = place("stageA", [1, D], BF16, 81920)
    stageB = place("stageB", [1, D], BF16, 83968)
    bvp_row = place("bvp_row", [1, D], F32, 86016)
    Wpq = place("Wpq", [128, NKC, D], BF16, 0)
    Wpk = place("Wpk", [128, NKC, D], BF16, 16384)
    qk_pre = place("qk_pre", [128, 4, 512], BF16, 32768)
    m_scr = place("m_scr", [128, 2, 512], F32, 40960)
    m_scrR = place("m_scrR", [128, 16, 64], F32, 53248)
    e_scr = place("e_scr", [128, 2, 512], F32, 57344)
    q2c = [place(f"q2c{i}", [128, D], BF16, 45056 + i * 2048) for i in range(2)]
    k2c = [place(f"k2c{i}", [128, D], BF16, 49152 + i * 2048) for i in range(2)]
    v2T = place("v2T", [128, NKC, T], BF16, 0)
    rqT = place("rqT", [16, T], F32, 90112)
    attnT = place("attnT", [128, NKC, T], BF16, 32768)
    rqbc_sb = place("rqbc_sb", [128, T], F32, 65536)
    rqTb = place("rqTb", [16, T], BF16, 73728)
    xn_rec = [place("xn_rec0", [128, 2, 512], F32, 0),
              place("xn_rec1", [128, 2, 512], F32, 24576)]
    y_scr = [place("y_scr0", [128, 2, 512], F32, 4096),
             place("y_scr1", [128, 2, 512], F32, 28672),
             place("y_scr2", [128, 2, 512], F32, 94208)]
    t2_scr = place("t2_scr", [128, 2, 512], F32, 8192)
    g_scr = place("g_scr", [128, 2, 512], F32, 12288)
    o1_scr = place("o1_scr", [128, 2, 512], F32, 16384)
    gam_t = place("gam_t", [128, D], F32, 20480)
    sq_scr = place("sq_scr", [128, 2, 512], F32, 90112)
    qsrc = [place("qsrc0", [128, 1, D], F32, 65536),
            place("qsrc1", [128, 1, D], F32, 69632)]
    qA = place("qA", [128, 1, D], F32, 73728)
    qB = place("qB", [128, 1, D], F32, 77824)
    qC = place("qC", [128, 1, D], F32, 81920)
    m_k = place("m_k", [128, 2, 512], F32, 36864)
    e_k = place("e_k", [128, 2, 512], F32, 61440)
    r_q2 = place("r_q2", [128, 2, 512], F32, 86016)
    r_k2 = place("r_k2", [128, 2, 512], F32, 90112)
    xr = [place(f"xr{i}", [128, 2, 512], F32, 65536 + i * 4096) for i in range(4)]
    ot = [place(f"ot{i}", [128, 2, 512], F32, 81920 + i * 4096) for i in range(2)]

    P = es.enter_context(nc.psum_tensor("P", [128, 8, 512], F32))

    sem = {}
    semnames = ["sT", "sV", "sS", "sG", "sPH", "hzV", "hzS"]
    semnames += [f"dW{i}" for i in range(4)]
    semnames += [f"dX{i}" for i in range(4)]
    semnames += [f"dR{i}" for i in range(4)]
    semnames += [f"dO{i}" for i in range(2)]
    for n in semnames:
        sem[n] = es.enter_context(nc.semaphore(n))
    cnt = {n: 0 for n in sem}

    def bump(n, k=1):
        cnt[n] += k
        return cnt[n]

    plan = {e: [] for e in ENGINES}
    LAST = {}
    HIST = {"vector": [], "scalar": []}
    vpad = A("vpad", [128, 1], F32)
    spad = A("spad", [128, 1], F32)

    def _nm(ap):
        try:
            return ap.tensor.name
        except AttributeError:
            return None

    def _names(aps):
        out = set()
        for ap in aps:
            n = _nm(ap)
            if n and not n.startswith("P"):
                out.add(n)
        return out

    def _fence(eng):
        hzn = "hzV" if eng == "vector" else "hzS"
        if eng == "vector":
            plan[eng].append(
                lambda e, s=sem[hzn]: e.memset(vpad[:, :], 0.0).then_inc(s, 1))
        else:
            plan[eng].append(
                lambda e, s=sem[hzn]: e.activation(
                    spad[:, :], spad[:, :], ACT.Identity,
                    scale=0.0).then_inc(s, 1))
        v = bump(hzn)
        WAIT(eng, hzn, v)
        HIST[eng].clear()

    def OP(eng, fn, inc=None, reads=(), writes=()):
        if eng in HIST:
            rd = _names(reads)
            if rd and any(rd & h for h in HIST[eng]):
                _fence(eng)

        def g(e, eng=eng, fn=fn, inc=inc):
            ins = fn(e)
            LAST[eng] = ins
            if inc is not None:
                ins.then_inc(sem[inc[0]], inc[1])
        plan[eng].append(g)
        if eng in HIST:
            HIST[eng].append(_names(writes))

    def WAIT(eng, sname, val):
        if val <= 0:
            return
        plan[eng].append(lambda e, s=sem[sname], v=val: e.wait_ge(s, v))

    def barrier():
        t = bump("sPH", 4)
        tcount = cnt["sT"]
        OP("vector", lambda en: en.engine_nop(), inc=("sPH", 1))
        OP("scalar", lambda en: en.activation(
            bar_s[:, :], bar_s[:, :], ACT.Identity, scale=0.0),
            inc=("sPH", 1))
        OP("gpsimd", lambda en: en.memset(bar_g[:, :], 0.0), inc=("sPH", 1))
        OP("sync", lambda en: en.nop(), inc=("sPH", 1))
        for e in ENGINES:
            WAIT(e, "sPH", t)
            if tcount > 0 and e != "tensor":
                WAIT(e, "sT", tcount)
        return t

    class DQ:

        def __init__(self, prefix, n, eng):
            self.names = [f"{prefix}{i}" for i in range(n)]
            self.n = n
            self.i = 0
            self.eng = eng

        def issue(self, dst, src, waits=()):
            k = self.i % self.n
            self.i += 1
            sname = self.names[k]
            if cnt[sname] > 0:
                WAIT(self.eng, sname, cnt[sname])
            for sn, v in waits:
                WAIT(self.eng, sn, v)
            OP(self.eng, lambda e, d=dst, s=src: e.dma_start(out=d, in_=s),
               inc=(sname, 16))
            return (sname, bump(sname, 16))

        def drain(self):
            for sname in self.names:
                if cnt[sname] > 0:
                    WAIT(self.eng, sname, cnt[sname])

    dqW = DQ("dW", 4, "gpsimd")
    dqX = DQ("dX", 4, "sync")
    dqR = DQ("dR", 4, "scalar")
    dqO = DQ("dO", 2, "sync")

    def mm(out, lhsT, rhs, start, stop, inc=None, tr=False):
        OP("tensor",
           lambda e, o=out, l=lhsT, r=rhs, s1=start, s2=stop, t=tr:
           e.matmul(o, l, r, start=s1, stop=s2, is_transpose=(True if t else None)),
           inc=inc)

    def vtt(out, a, b, op, inc=None, eng="vector"):
        OP(eng, lambda e, o=out, x=a, y=b, p=op: e.tensor_tensor(o, x, y, p),
           inc=inc, reads=(a, b), writes=(out,))

    def vts(out, in0, s1, s2, op0, op1=ALU.bypass, inc=None, eng="vector"):
        rd = [in0] + [s for s in (s1, s2) if not isinstance(s, (int, float))
                      and s is not None]
        OP(eng,
           lambda e, o=out, i=in0, a=s1, b=s2, p0=op0, p1=op1:
           e.tensor_scalar(o, i, a, b, p0, p1), inc=inc,
           reads=tuple(rd), writes=(out,))

    def vstt(out, in0, sc, in1, op0, op1, accum=None, inc=None):
        rd = [in0, in1] + ([sc] if not isinstance(sc, (int, float)) else [])
        OP("vector",
           lambda e, o=out, i0=in0, s=sc, i1=in1, p0=op0, p1=op1, ac=accum:
           e.scalar_tensor_tensor(o, i0, s, i1, p0, p1, accum_out=ac),
           inc=inc, reads=tuple(rd), writes=(out, accum) if accum is not None
           else (out,))

    def vred(out, in_, op, absval=None, inc=None, axis=None):
        ax = axis if axis is not None else AXL.X
        OP("vector",
           lambda e, o=out, i=in_, p=op, av=absval, a=ax:
           e.tensor_reduce(o, i, a, p, apply_absolute_value=av), inc=inc,
           reads=(in_,), writes=(out,))

    def vcopy(out, in_, inc=None, eng="vector"):
        if eng == "scalar":
            OP(eng, lambda e, o=out, i=in_: e.copy(o, i), inc=inc,
               reads=(in_,), writes=(out,))
        else:
            OP(eng, lambda e, o=out, i=in_: e.tensor_copy(o, i), inc=inc,
               reads=(in_,), writes=(out,))

    def vrecip(out, in_, inc=None):
        OP("vector", lambda e, o=out, i=in_: e.reciprocal(o, i), inc=inc,
           reads=(in_,), writes=(out,))

    def sact(out, in_, func, bias=0.0, scale=1.0, accum=None, inc=None):
        rd = [in_] + [s for s in (bias, scale) if not isinstance(s, (int, float))]
        OP("scalar",
           lambda e, o=out, i=in_, f=func, b=bias, s=scale, ac=accum:
           e.activation(o, i, f, bias=b, scale=s, accum_out=ac), inc=inc,
           reads=tuple(rd), writes=(out,))

    def scopy(out, in_, inc=None):
        OP("scalar", lambda e, o=out, i=in_: e.copy(o, i), inc=inc,
           reads=(in_,), writes=(out,))

    class _Stop(Exception):
        pass

    class _Stop(Exception):
        pass

    def _phases():
        OP("vector", lambda e: e.memset(vpad[:, :], 1.0))
        vtt(vpad[:, :], vpad[:, :], vpad[:, :], ALU.mult)
        vtt(vpad[:, :], vpad[:, :], vpad[:, :], ALU.add)
        OP("gpsimd", lambda e: e.memset(ident_b[:, :], 0.0))
        OP("gpsimd", lambda e: e.memset(ident_f[:, :], 0.0))
        OP("gpsimd", lambda e: e.memset(ones_col[:, :], 1.0))
        OP("gpsimd", lambda e: e.memset(ones_f[:, :], 1.0))
        OP("gpsimd", lambda e: e.memset(ones_row_f[:, :], 1.0))
        OP("gpsimd", lambda e: e.memset(selp[:, :, :, :], 0.0), inc=("sG", 1))
        g_ms = bump("sG")
        WAIT("gpsimd", "sG", g_ms)
        OP("gpsimd", lambda e: e.affine_select(
            out=ident_b[:, :], in_=ident_b[:, :], compare_op=ALU.not_equal,
            fill=1.0, base=0, pattern=[[-1, 128]], channel_multiplier=1),
            inc=("sG", 1))
        WAIT("gpsimd", "sG", bump("sG"))
        OP("gpsimd", lambda e: e.affine_select(
            out=ident_f[:, :], in_=ident_f[:, :], compare_op=ALU.not_equal,
            fill=1.0, base=0, pattern=[[-1, 128]], channel_multiplier=1),
            inc=("sG", 1))
        WAIT("gpsimd", "sG", bump("sG"))
        OP("gpsimd", lambda e: e.affine_select(
            out=selp[:, :, :, :], in_=selp[:, :, :, :], compare_op=ALU.not_equal,
            fill=1.0, base=0, pattern=[[-2, NPAIR], [-1, 2], [0, 64]],
            channel_multiplier=1), inc=("sG", 1))
        g_ident = bump("sG")

        wv = {}

        def gdma(name, dst, src, waits=()):
            wv[name] = dqW.issue(dst, src, waits)

        gdma("bo", bo_row[:, :], rowr("bo"))
        gdma("gamma", gamma_row[:, :], rowr("gamma"))
        gdma("fbias", fbias_row[:, :], rowr("fbias"))
        gdma("beta", beta_row[:, :], rowr("beta"))
        gdma("fa", fa_sb[:, :, :], dd["fa"].rearrange("(c p) r -> p c r", p=128))
        gdma("fb", fb_sb[:, :], dd["fb"][:, :])
        gdma("Wo", Wo_sb[:, :, :], wr("Wo"))

        WAIT("vector", *wv["fbias"])
        WAIT("vector", *wv["beta"])
        vtt(cfb_f32[:, :], fbias_row[:, :], beta_row[:, :], ALU.add)
        vcopy(cfb_row[:, :], cfb_f32[:, :])

        xin = {}
        xn_done = {}
        ssum = {}
        wready = {}
        rs0r = {}
        sqr = {}
        ssqred = {}
        tr2 = {}
        evac_rec = {}

        def emit_tr(c):
            bp = (c % 2) * 2
            if c == 0:
                WAIT("tensor", "sG", g_ident)
            WAIT("tensor", *xn_done[c])
            if c >= 2:
                WAIT("tensor", evac_rec[c - 2][0], evac_rec[c - 2][1])
            for half in range(2):
                for k in range(4):
                    kc = 4 * half + k
                    last = (half == 1 and k == 3)
                    mm(P[:, bp + half:bp + half + 1, k * 128:(k + 1) * 128],
                       xn_tm[:, c:c + 1, kc * 128:(kc + 1) * 128],
                       ident_b[:, :], start=(k == 0), stop=(k == 3),
                       inc=("sT", 1) if last else None)
            tr2[c] = bump("sT")

        def emit_evac(c):
            bp = (c % 2) * 2
            eng = "scalar" if c % 2 == 0 else "vector"
            WAIT(eng, "sT", tr2[c])
            vcopy(xnT[:, 0:4, c * 128:(c + 1) * 128], P[:, bp:bp + 1, :],
                  eng=eng)
            sn = "sS" if eng == "scalar" else "sV"
            vcopy(xnT[:, 4:8, c * 128:(c + 1) * 128], P[:, bp + 1:bp + 2, :],
                  eng=eng, inc=(sn, 1))
            evac_rec[c] = (sn, bump(sn))
        for b in range(4):
            cs = list(range(4 * b, 4 * b + 4))
            for c in cs:
                w = [xn_done[c - 8]] if c >= 8 else []
                xin[c] = dqX.issue(xt[c % 8][:, :],
                                   x_d[c * 128:(c + 1) * 128, :], waits=w)
            for c in cs:
                WAIT("scalar", *xin[c])
                if c >= 2:
                    WAIT("scalar", "sV", ssqred[c - 2])
                sact([garb_s, garb_v][c % 2][:, :], xt[c % 8][:, :],
                     ACT.Square, inc=("sS", 1))
                sqr[c] = bump("sS")
                WAIT("vector", *xin[c])
                WAIT("vector", "sS", sqr[c])
                vred(ssq1_sb[:, c:c + 1], [garb_s, garb_v][c % 2][:, :],
                     ALU.add, inc=("sV", 1))
                ssqred[c] = bump("sV")
                vred(s1_sb[:, c:c + 1], xt[c % 8][:, :], ALU.add)
                vts(nm1_sb[:, c:c + 1], s1_sb[:, c:c + 1], -1.0 / D, None, ALU.mult)
                vts(tmpA[:, :], ssq1_sb[:, c:c + 1], 1.0 / D, None, ALU.mult)
                vtt(tmpB[:, :], nm1_sb[:, c:c + 1], nm1_sb[:, c:c + 1], ALU.mult)
                vtt(tmpC[:, :], tmpA[:, :], tmpB[:, :], ALU.subtract)
                vts(w1_sb[:, c:c + 1], tmpC[:, :], 1e-5, None, ALU.add,
                    inc=("sV", 1))
                wready[c] = bump("sV")
            for c in cs:
                WAIT("scalar", "sV", wready[c])
                sact(tmpS[:, :], w1_sb[:, c:c + 1], ACT.Ln)
                sact(rs0_sb[:, c:c + 1], tmpS[:, :], ACT.Exp, scale=-0.5,
                     inc=("sS", 1))
                rs0r[c] = bump("sS")
            for c in cs:
                WAIT("vector", "sS", rs0r[c])
                vtt(tmpA[:, :], rs0_sb[:, c:c + 1], rs0_sb[:, c:c + 1], ALU.mult)
                vtt(tmpB[:, :], w1_sb[:, c:c + 1], tmpA[:, :], ALU.mult)
                vts(tmpC[:, :], tmpB[:, :], -0.5, 1.5, ALU.mult, ALU.add)
                vtt(rs1_sb[:, c:c + 1], rs0_sb[:, c:c + 1], tmpC[:, :], ALU.mult)
                vtt(nmrs1_sb[:, c:c + 1], nm1_sb[:, c:c + 1], rs1_sb[:, c:c + 1],
                    ALU.mult, inc=("sV", 1))
                nwd = bump("sV")
                WAIT("scalar", "sV", nwd)
                sact(xn_tm[:, c:c + 1, :], xt[c % 8][:, :], ACT.Identity,
                     bias=nmrs1_sb[:, c:c + 1], scale=rs1_sb[:, c:c + 1],
                     inc=("sS", 1))
                xn_done[c] = ("sS", bump("sS"))
                emit_tr(c)
                if c >= 1:
                    emit_evac(c - 1)
        emit_evac(NTC - 1)

        if upto <= 1:
            barrier()
            raise _Stop()
        if upto <= 2:
            barrier()
            raise _Stop()

        for ic in range(8):
            gdma("F0_%d" % ic, F_w[:, ic:ic + 1, :], wr("Wq")[:, ic:ic + 1, :],
                 waits=[("sT", tr2[ic])])

        d_fk0 = dqX.issue(fkh[0][:, :, :], wr("fk")[:, 0:4, :],
                          waits=[xn_done[11]])
        d_fk1 = dqX.issue(fkh[1][:, :, :], wr("fk")[:, 4:8, :],
                          waits=[xn_done[15]])
        for h in range(2):
            WAIT("vector", *(d_fk0 if h == 0 else d_fk1))
            vred(qs4_sb[:, :], fkh[h][:, :, :], ALU.max, absval=True)
            vred(qs_sb[:, h:h + 1], qs4_sb[:, :], ALU.max)
        vtt(qs_sb[:, 2:3], qs_sb[:, 0:1], qs_sb[:, 1:2], ALU.max, inc=("sV", 1))
        v_qmax = bump("sV")
        WAIT("tensor", "sV", v_qmax)
        WAIT("tensor", "sG", g_ident)
        WAIT("tensor", evac_rec[14][0], evac_rec[14][1])
        WAIT("tensor", evac_rec[15][0], evac_rec[15][1])
        mm(P[0:1, 2:3, 0:128], qs_sb[:, 2:3], ident_f[:, :], start=True,
           stop=True, inc=("sT", 1))
        t_qt = bump("sT")
        WAIT("vector", "sT", t_qt)
        vcopy(qrow[0:1, :], P[0:1, 2:3, 0:128])
        vred(qs_sb[0:1, 3:4], qrow[0:1, :], ALU.max, inc=("sV", 1))
        v_qrow = bump("sV")
        WAIT("tensor", "sV", v_qrow)
        mm(P[:, 2:3, 0:1], ones_row_f[0:1, :], qs_sb[0:1, 3:4], start=True,
           stop=True, inc=("sT", 1))
        t_qb = bump("sT")
        WAIT("vector", "sT", t_qb)
        vcopy(qs_sb[:, 3:4], P[:, 2:3, 0:1])
        OP("vector", lambda e: e.memset(tmpA[:, :], 7.0))
        vrecip(tmpB[:, :], tmpA[:, :])
        vtt(qs_sb[:, 4:5], qs_sb[:, 3:4], tmpB[:, :], ALU.mult)
        vrecip(qs_sb[:, 5:6], qs_sb[:, 4:5])
        vts(qs_sb[:, 6:7], qs_sb[:, 4:5], 8.0, None, ALU.mult)

        for ic in range(8):
            gdma("E0_%d" % ic, E_w[:, ic:ic + 1, :],
                 wr("Wq2")[:, ic:ic + 1, :], waits=[("sT", tr2[8 + ic])])

        WAIT("tensor", *wv["fa"])
        WAIT("tensor", "sG", g_ident)
        fat_g = {}
        for c in range(8):
            bank = 6 + c // 4
            mm(P[0:16, bank:bank + 1, (c % 4) * 128:(c % 4) * 128 + 128],
               fa_sb[:, c:c + 1, :], ident_b[:, :],
               start=(c % 4 == 0), stop=(c % 4 == 3),
               inc=("sT", 1) if c % 4 == 3 else None)
            if c % 4 == 3:
                fat_g[c // 4] = bump("sT")
        for i in range(2):
            WAIT("vector", "sT", fat_g[i])
            vcopy(faT[0:16, i * 512:(i + 1) * 512], P[0:16, 6 + i:7 + i, :],
                  inc=("sV", 1))
            v_fat = bump("sV")

        gdma("stA0", stageA[:, :], rowr("bq"), waits=[("sV", ssqred[15])])
        gdma("stB0", stageB[:, :], rowr("bq2"))

        barrier()
        if upto <= 3:
            raise _Stop()

        WEIGHTS = [
            ("q", "Wq", "Wq2", "bq", "bq2", Wpq, ("row", bqp_row)),
            ("k", "Wk", "Wk2", "bk", "bk2", Wpk, ("row", bkp_row)),
            ("v", "Wv", "Wv2", "bv", "bv2", Wpv, ("col", bvp_col)),
        ]
        tr_done = {}
        col_done = {}
        bias_done = {}
        prod_done = {}
        prod_evac_done = {}
        for wi, (tag, w1n, w2n, b1n, b2n, Wp_dst, brow) in enumerate(WEIGHTS):
            if wi >= 1:
                gdma("F%da" % wi, F_w[:, 0:4, :], wr(w1n)[:, 0:4, :],
                     waits=[("sT", tr_done[wi - 1])])
                gdma("F%d" % wi, F_w[:, 4:8, :], wr(w1n)[:, 4:8, :],
                     waits=[("sT", tr_done[wi - 1])])
                gdma("E%da" % wi, E_w[:, 0:4, :], wr(w2n)[:, 0:4, :],
                     waits=[("sT", prod_done[wi - 1])])
                gdma("E%d" % wi, E_w[:, 4:8, :], wr(w2n)[:, 4:8, :],
                     waits=[("sT", prod_done[wi - 1])])
                gdma("stA%d" % wi, stageA[:, :], rowr(b1n),
                     waits=[("sT", col_done[wi - 1])])
                gdma("stB%d" % wi, stageB[:, :], rowr(b2n),
                     waits=[("sT", bias_done[wi - 1])])
            fk_ = "F%d" % wi if wi >= 1 else "F0"
            ek_ = "E%d" % wi if wi >= 1 else "E0"
            if wi == 0:
                for ic in range(4, 8):
                    WAIT("tensor", *wv["F0_%d" % ic])
            else:
                WAIT("tensor", *wv[fk_])
                WAIT("tensor", *wv[fk_ + "a"])
            if wi >= 1:
                WAIT("tensor", "sS", prod_evac_done[wi - 1])
            wt_ev = {}
            for kc in range(8):
                for ich in range(2):
                    g = kc * 2 + ich
                    bank = g % 2
                    if g >= 2:
                        WAIT("tensor", "sV", wt_ev[g - 2])
                    for k in range(4):
                        ic = ich * 4 + k
                        mm(P[:, bank:bank + 1, k * 128:(k + 1) * 128],
                           F_w[:, ic:ic + 1, kc * 128:(kc + 1) * 128],
                           ident_b[:, :], start=(k == 0), stop=(k == 3),
                           inc=("sT", 1) if k == 3 else None)
                    t_g = bump("sT")
                    WAIT("vector", "sT", t_g)
                    vcopy(WT_w[:, kc:kc + 1, ich * 512:(ich + 1) * 512],
                          P[:, bank:bank + 1, :], inc=("sV", 1))
                    wt_ev[g] = bump("sV")
            tr_done[wi] = cnt["sT"]
            WAIT("tensor", *wv["stA%d" % wi])
            for kcc in range(8):
                mm(P[:, 2:3, kcc:kcc + 1],
                   stageA[0:1, kcc * 128:(kcc + 1) * 128], ones_col[0:1, 0:1],
                   start=(kcc == 0), stop=(kcc == 7),
                   inc=("sT", 1) if kcc == 7 else None)
            col_done[wi] = bump("sT")
            WAIT("vector", "sT", col_done[wi])
            vcopy(bcol_st[:, 0:8], P[:, 2:3, 0:8], inc=("sV", 1))
            v_col = bump("sV")
            WAIT("tensor", "sV", v_col)
            if wi == 0:
                for ic in range(4, 8):
                    WAIT("tensor", *wv["E0_%d" % ic])
            else:
                WAIT("tensor", *wv[ek_])
                WAIT("tensor", *wv[ek_ + "a"])
            WAIT("tensor", *wv["stB%d" % wi])
            for jh in range(2):
                bank = 4 + jh
                for kcc in range(8):
                    mm(P[0:1, bank:bank + 1, :], bcol_st[:, kcc:kcc + 1],
                       E_w[:, kcc:kcc + 1, jh * 512:(jh + 1) * 512],
                       start=(kcc == 0), stop=False)
                mm(P[0:1, bank:bank + 1, :], ones_col[0:1, 0:1],
                   stageB[0:1, jh * 512:(jh + 1) * 512], start=False, stop=True,
                   inc=("sT", 1) if jh == 1 else None)
            bias_done[wi] = bump("sT")
            WAIT("scalar", "sT", bias_done[wi])
            if brow[0] == "row":
                for jh in range(2):
                    scopy(brow[1][0:1, jh * 512:(jh + 1) * 512],
                          P[0:1, 4 + jh:5 + jh, :],
                          inc=("sS", 1) if jh == 1 else None)
                s_brow = bump("sS")
            else:
                for jh in range(2):
                    scopy(bvp_row[0:1, jh * 512:(jh + 1) * 512],
                          P[0:1, 4 + jh:5 + jh, :],
                          inc=("sS", 1) if jh == 1 else None)
                s_brow = bump("sS")
                WAIT("tensor", "sS", s_brow)
                for kcc in range(8):
                    mm(P[:, 6:7, kcc:kcc + 1],
                       bvp_row[0:1, kcc * 128:(kcc + 1) * 128], ones_f[0:1, 0:1],
                       start=(kcc == 0), stop=(kcc == 7),
                       inc=("sT", 1) if kcc == 7 else None)
                t_bvp = bump("sT")
                WAIT("vector", "sT", t_bvp)
                vcopy(bvp_col[:, 0:8], P[:, 6:7, 0:8], inc=("sV", 1))
                bump("sV")
            WAIT("tensor", "sV", wt_ev[15])
            WAIT("tensor", "sS", cnt["sS"])
            if brow[0] == "col":
                WAIT("tensor", "sV", cnt["sV"])
            pe_v = {}
            for jh in range(2):
                if jh == 1:
                    WAIT("tensor", "sS", pe_v[0])
                for kc in range(8):
                    for ic in range(8):
                        mm(P[:, ic:ic + 1, :],
                           WT_w[:, kc:kc + 1, ic * 128:(ic + 1) * 128],
                           E_w[:, kc:kc + 1, jh * 512:(jh + 1) * 512],
                           start=(kc == 0), stop=(kc == 7),
                           inc=("sT", 1) if (kc == 7 and ic == 7) else None)
                t_pj = bump("sT")
                WAIT("scalar", "sT", t_pj)
                for ic in range(8):
                    scopy(Wp_dst[:, ic:ic + 1, jh * 512:(jh + 1) * 512],
                          P[:, ic:ic + 1, :], inc=("sS", 1) if ic == 7 else None)
                pe_v[jh] = bump("sS")
                if jh == 1:
                    prod_done[wi] = t_pj
                    prod_evac_done[wi] = pe_v[1]

        barrier()
        if upto <= 4:
            raise _Stop()

        q2pre_v = {}
        k2pre_v = {}
        evq = {}
        evk = {}
        q2cd = {}
        k2cd = {}
        qm_g = {}
        vq_done = {}
        S_done = {}
        GE = ALU.is_ge
        q_rinv = qs_sb[:, 5:6]
        q_scale = qs_sb[:, 4:5]
        q_s8 = qs_sb[:, 6:7]
        d_q8 = {}
        lq_done = {}
        for qi in range(2):
            d_q8[qi] = dqX.issue(qsrc[qi][:, :, :], wr("fk")[:, qi:qi + 1, :])

        def gtt2(out, a, b, op, inc=None):
            OP("gpsimd",
               lambda e, o=out, x=a, y=b, p=op: e.tensor_tensor(o, x, y, p),
               inc=inc)

        def quant_part(qi, half):
            wdst = Weff[:, qi:qi + 1, :]
            S = qsrc[qi % 2][:, :, :]
            if half == 0:
                WAIT("vector", *d_q8[qi])
                vts(qA[:, :, :], S, q_rinv, 8.5, ALU.mult, ALU.add)
                vts(qB[:, :, :], qA[:, :, :], 8.0, None, GE)
                vstt(qC[:, :, :], qB[:, :, :], -8.0, qA[:, :, :],
                     ALU.mult, ALU.add)
                vts(qB[:, :, :], qC[:, :, :], 4.0, None, GE)
                vstt(S, qB[:, :, :], -4.0, qC[:, :, :], ALU.mult, ALU.add)
                vts(qB[:, :, :], S, 2.0, None, GE)
            else:
                vstt(qC[:, :, :], qB[:, :, :], -2.0, S, ALU.mult, ALU.add)
                vts(qB[:, :, :], qC[:, :, :], 1.0, None, GE)
                vstt(S, qB[:, :, :], -1.0, qC[:, :, :], ALU.mult, ALU.add)
                vtt(qC[:, :, :], qA[:, :, :], S, ALU.subtract)
                vts(wdst, qC[:, :, :], q_scale, q_s8, ALU.mult,
                    ALU.subtract, inc=("sV", 1))
                lq_done[qi] = ("sV", bump("sV"))

        for c in range(NTC):
            if c >= 1:
                WAIT("tensor", "sS", evk[c - 1])
            for kc in range(8):
                lh = xnT[:, kc:kc + 1, c * 128:(c + 1) * 128]
                mm(P[:, 0:1, :], lh, Wpq[:, kc:kc + 1, 0:512],
                   start=(kc == 0), stop=False)
                mm(P[:, 1:2, :], lh, Wpq[:, kc:kc + 1, 512:1024],
                   start=(kc == 0), stop=False)
                mm(P[:, 2:3, :], lh, Wpk[:, kc:kc + 1, 0:512],
                   start=(kc == 0), stop=False)
                mm(P[:, 3:4, :], lh, Wpk[:, kc:kc + 1, 512:1024],
                   start=(kc == 0), stop=False)
            mm(P[:, 0:1, :], ones_col[0:1, :], bqp_row[0:1, 0:512],
               start=False, stop=True)
            mm(P[:, 1:2, :], ones_col[0:1, :], bqp_row[0:1, 512:1024],
               start=False, stop=True, inc=("sT", 1))
            q2pre_v[c] = bump("sT")
            mm(P[:, 2:3, :], ones_col[0:1, :], bkp_row[0:1, 0:512],
               start=False, stop=True)
            mm(P[:, 3:4, :], ones_col[0:1, :], bkp_row[0:1, 512:1024],
               start=False, stop=True, inc=("sT", 1))
            k2pre_v[c] = bump("sT")
            if c >= 1:
                cp = c - 1
                WAIT("tensor", "sG", q2cd[cp])
                WAIT("tensor", "sG", k2cd[cp])
                for p in range(8):
                    bank = 4 + p // 4
                    mm(P[:, bank:bank + 1, (p % 4) * 128:(p % 4) * 128 + 128],
                       k2c[cp % 2][:, p * 128:(p + 1) * 128],
                       q2c[cp % 2][:, p * 128:(p + 1) * 128],
                       start=(cp == 0 and p % 4 == 0), stop=False,
                       inc=("sT", 1) if p == 7 else None)
                S_done[cp] = bump("sT")
            WAIT("scalar", "sT", q2pre_v[c])
            scopy(qk_pre[:, 0:2, :], P[:, 0:2, :], inc=("sS", 1))
            evq[c] = bump("sS")
            WAIT("scalar", "sT", k2pre_v[c])
            scopy(qk_pre[:, 2:4, :], P[:, 2:4, :], inc=("sS", 1))
            evk[c] = bump("sS")
            if c >= 1:
                WAIT("scalar", "sG", k2cd[c - 1])
            sact(m_scr[:, :, :], qk_pre[:, 0:2, :], ACT.Relu, scale=-1.0)
            sact(m_k[:, :, :], qk_pre[:, 2:4, :], ACT.Relu, scale=-1.0)
            sact(r_q2[:, :, :], qk_pre[:, 0:2, :], ACT.Relu)
            sact(r_k2[:, :, :], qk_pre[:, 2:4, :], ACT.Relu)
            sact(e_scr[:, :, :], m_scr[:, :, :], ACT.Exp, scale=-1.0,
                 inc=("sS", 1))
            s_eq = bump("sS")
            sact(e_k[:, :, :], m_k[:, :, :], ACT.Exp, scale=-1.0,
                 inc=("sS", 1))
            s_ek = bump("sS")
            if c >= 2:
                WAIT("gpsimd", "sT", S_done[c - 2])
            WAIT("gpsimd", "sS", s_eq)
            gtt2(q2c[c % 2][:, :], e_scr[:, :, :], r_q2[:, :, :], ALU.add,
                 inc=("sG", 1))
            q2cd[c] = bump("sG")
            WAIT("gpsimd", "sS", s_ek)
            gtt2(k2c[c % 2][:, :], e_k[:, :, :], r_k2[:, :, :], ALU.add,
                 inc=("sG", 1))
            k2cd[c] = bump("sG")
            if c >= 1:
                WAIT("gpsimd", "sV", vq_done[c - 1])
            WAIT("gpsimd", "sG", k2cd[c])
            gtt2(m_scrR[:, :, :], q2c[c % 2][:, :], k2c[c % 2][:, :],
                 ALU.mult, inc=("sG", 1))
            qm_g[c] = bump("sG")
            WAIT("vector", "sG", qm_g[c])
            vred(qks_sb[:, c * 16:(c + 1) * 16], m_scrR[:, :, :], ALU.add,
                 inc=("sV", 1))
            vq_done[c] = bump("sV")
            quant_part(c // 2, c % 2)
            if c % 2 == 1 and c < 12:
                qn = c // 2 + 2
                d_q8[qn] = dqX.issue(qsrc[qn % 2][:, :, :],
                                     wr("fk")[:, qn:qn + 1, :],
                                     waits=[lq_done[c // 2]])
        cp = NTC - 1
        WAIT("tensor", "sG", q2cd[cp])
        WAIT("tensor", "sG", k2cd[cp])
        for p in range(8):
            bank = 4 + p // 4
            mm(P[:, bank:bank + 1, (p % 4) * 128:(p % 4) * 128 + 128],
               k2c[cp % 2][:, p * 128:(p + 1) * 128],
               q2c[cp % 2][:, p * 128:(p + 1) * 128],
               start=False, stop=(p % 4 == 3),
               inc=("sT", 1) if p == 7 else None)
        S_done[cp] = bump("sT")
        WAIT("tensor", "sV", v_fat)
        WAIT("tensor", *wv["fb"])
        WAIT("tensor", *lq_done[7])
        wfev = {}
        for ic in range(8):
            for jf in range(2):
                g = ic * 2 + jf
                bank = 6 + g % 2
                if g >= 2:
                    WAIT("tensor", "sS", wfev[g - 2])
                mm(P[:, bank:bank + 1, :], faT[0:16, ic * 128:(ic + 1) * 128],
                   fb_sb[0:16, jf * 512:(jf + 1) * 512], start=True, stop=False)
                mm(P[:, bank:bank + 1, :], ident_b[:, :],
                   Weff[:, ic:ic + 1, jf * 512:(jf + 1) * 512],
                   start=False, stop=True, inc=("sT", 1))
                t_wf = bump("sT")
                WAIT("scalar", "sT", t_wf)
                scopy(Weff[:, ic:ic + 1, jf * 512:(jf + 1) * 512],
                      P[:, bank:bank + 1, :], inc=("sS", 1))
                wfev[g] = bump("sS")
        vts(rqA_sb[:, :], qks_sb[:, :], 1e-6, None, ALU.add)
        vrecip(rqB_sb[:, :], rqA_sb[:, :])
        OP("vector", lambda e: e.engine_nop(), inc=("sV", 1))
        v_rq = bump("sV")

        barrier()
        if upto <= 5:
            raise _Stop()

        rqt_ev = {}
        WAIT("tensor", "sV", v_rq)
        for g in range(4):
            bank = 6 + g % 2
            if g >= 2:
                WAIT("tensor", "sV", rqt_ev[g - 2])
            for j in range(4):
                tc = g * 4 + j
                mm(P[0:16, bank:bank + 1, j * 128:(j + 1) * 128],
                   rqB_sb[:, tc * 16:(tc + 1) * 16], ident_f[:, :],
                   start=(j == 0), stop=(j == 3), tr=True,
                   inc=("sT", 1) if j == 3 else None)
            t_rg = bump("sT")
            WAIT("vector", "sT", t_rg)
            vcopy(rqT[0:16, g * 512:(g + 1) * 512], P[0:16, bank:bank + 1, :],
                  inc=("sV", 1))
            rqt_ev[g] = bump("sV")
        v2ev = {}
        for mc in range(8):
            for u in range(2):
                unit = mc * 2 + u
                bs = (unit % 2) * 2
                if unit >= 2:
                    WAIT("tensor", "sS", v2ev[unit - 2])
                for kc in range(8):
                    for t2 in range(2):
                        th = u * 2 + t2
                        mm(P[:, bs + t2:bs + t2 + 1, :],
                           Wpv[:, kc:kc + 1, mc * 128:(mc + 1) * 128],
                           xnT[:, kc:kc + 1, th * 512:(th + 1) * 512],
                           start=(kc == 0), stop=(kc == 7),
                           inc=("sT", 1) if (kc == 7 and t2 == 1) else None)
                t_u = bump("sT")
                WAIT("scalar", "sT", t_u)
                for t2 in range(2):
                    th = u * 2 + t2
                    sact(v2T[:, mc:mc + 1, th * 512:(th + 1) * 512],
                         P[:, bs + t2:bs + t2 + 1, :], ACT.Identity,
                         bias=bvp_col[:, mc:mc + 1],
                         inc=("sS", 1) if t2 == 1 else None)
                v2ev[unit] = bump("sS")

        barrier()
        if upto <= 6:
            raise _Stop()

        OP("vector", lambda e: e.memset(Sbd[:, :, :], 0.0))
        for p in range(8):
            bank = 4 + p // 4
            off = (p % 4) * 128
            vcopy(Sbd[0:64, p:p + 1, 0:64], P[0:64, bank:bank + 1, off:off + 64])
            vcopy(Sbd[64:128, p:p + 1, 64:128],
                  P[64:128, bank:bank + 1, off + 64:off + 128],
                  inc=("sV", 1) if p == 7 else None)
        v_sbd = bump("sV")
        WAIT("vector", "sV", rqt_ev[3])
        vcopy(rqTb[0:16, :], rqT[0:16, :], inc=("sV", 1))
        v_rqtb = bump("sV")
        norm = {}
        rqev = {}
        for p in range(8):
            if p == 0:
                WAIT("tensor", "sV", v_sbd)
                WAIT("tensor", "sV", v_rqtb)
            if p >= 1:
                WAIT("tensor", "sS", rqev[p - 1])
            for th in range(4):
                mm(P[:, 4 + th:5 + th, :], selp[0:16, p:p + 1, :, :],
                   rqTb[0:16, th * 512:(th + 1) * 512], start=True, stop=True,
                   inc=("sT", 1) if th == 3 else None)
            t_rb = bump("sT")
            WAIT("scalar", "sT", t_rb)
            if p >= 1:
                WAIT("scalar", "sV", norm[p - 1])
            for th in range(4):
                scopy(rqbc_sb[:, th * 512:(th + 1) * 512], P[:, 4 + th:5 + th, :],
                      inc=("sS", 1) if th == 3 else None)
            rqev[p] = bump("sS")
            if p >= 1:
                WAIT("tensor", "sV", norm[p - 1])
            for th in range(4):
                mm(P[:, th:th + 1, :], Sbd[:, p:p + 1, :],
                   v2T[:, p:p + 1, th * 512:(th + 1) * 512], start=True, stop=True,
                   inc=("sT", 1) if th == 3 else None)
            t_o = bump("sT")
            WAIT("vector", "sT", t_o)
            WAIT("vector", "sS", rqev[p])
            for th in range(4):
                vtt(attnT[:, p:p + 1, th * 512:(th + 1) * 512],
                    P[:, th:th + 1, :],
                    rqbc_sb[:, th * 512:(th + 1) * 512], ALU.mult,
                    inc=("sV", 1) if th == 3 else None)
            norm[p] = bump("sV")

        barrier()
        if upto <= 7:
            raise _Stop()

        WAIT("tensor", *wv["gamma"])
        for jh in range(2):
            mm(P[:, 6 + jh:7 + jh, :], ones_row_f[0:1, :],
               gamma_row[0:1, jh * 512:(jh + 1) * 512], start=True, stop=True,
               inc=("sT", 1) if jh == 1 else None)
        t_gt = bump("sT")
        WAIT("vector", "sT", t_gt)
        vcopy(gam_t[:, 0:512], P[:, 6:7, :])
        vcopy(gam_t[:, 512:1024], P[:, 7:8, :], inc=("sV", 1))
        v_gt = bump("sV")

        xrin = {}
        for i in range(4):
            xrin[i] = dqR.issue(xr[i][:, :, :], x_d[i * 128:(i + 1) * 128, :])

        ymm = {}
        fmm = {}
        xnr = {}
        yused = {}
        o1used = {}
        outt_v = {}
        s2r = {}
        w2r = {}
        r02 = {}
        nrm2 = {}
        t2r = {}
        gdone = {}
        odma = {}
        for s in range(NTC + 2):
            c1 = s
            c2 = s - 1
            c3 = s - 2
            if c1 < NTC:
                c = c1
                b0 = (c % 2) * 2
                if c >= 2:
                    WAIT("tensor", "sV", yused[c - 2])
                for jh in range(2):
                    for kc in range(8):
                        mm(P[:, b0 + jh:b0 + jh + 1, :],
                           attnT[:, kc:kc + 1, c * 128:(c + 1) * 128],
                           Wo_sb[:, kc:kc + 1, jh * 512:(jh + 1) * 512],
                           start=(kc == 0), stop=False)
                    mm(P[:, b0 + jh:b0 + jh + 1, :], ones_col[0:1, :],
                       bo_row[0:1, jh * 512:(jh + 1) * 512], start=False,
                       stop=True, inc=("sT", 1) if jh == 1 else None)
                ymm[c] = bump("sT")
            if 0 <= c2 < NTC:
                c = c2
                b0 = (c % 2) * 2
                if c >= 2:
                    WAIT("tensor", "sV", o1used[c - 2])
                if c == 1:
                    WAIT("tensor", "sV", v_gt)
                for jh in range(2):
                    for kc in range(8):
                        mm(P[:, 4 + b0 + jh:5 + b0 + jh, :],
                           xnT[:, kc:kc + 1, c * 128:(c + 1) * 128],
                           Weff[:, kc:kc + 1, jh * 512:(jh + 1) * 512],
                           start=(kc == 0), stop=False)
                    mm(P[:, 4 + b0 + jh:5 + b0 + jh, :], ones_col[0:1, :],
                       cfb_row[0:1, jh * 512:(jh + 1) * 512], start=False,
                       stop=True, inc=("sT", 1) if jh == 1 else None)
                fmm[c] = bump("sT")
            if c1 < NTC:
                c = c1
                WAIT("scalar", *xrin[c])
                if c >= 2:
                    WAIT("scalar", "sV", yused[c - 2])
                sact(xn_rec[c % 2][:, :, :], xr[c % 4][:, :, :], ACT.Identity,
                     bias=nmrs1_sb[:, c:c + 1], scale=rs1_sb[:, c:c + 1],
                     inc=("sS", 1))
                xnr[c] = bump("sS")
            if 0 <= c2 < NTC:
                c = c2
                WAIT("scalar", "sV", yused[c])
                sact(sq_scr[:, :, :], y_scr[c % 3][:, :, :], ACT.Square,
                     inc=("sS", 1))
                s2r[c] = bump("sS")
                WAIT("vector", "sS", s2r[c])
                vred(qs4_sb[:, 0:2], y_scr[c % 3][:, :, :], ALU.add)
                vred(s2_sb[:, c:c + 1], qs4_sb[:, 0:2], ALU.add)
                vred(qs4_sb[:, 2:4], sq_scr[:, :, :], ALU.add)
                vred(ssq2_sb[:, c:c + 1], qs4_sb[:, 2:4], ALU.add)
                vts(nm2_sb[:, c:c + 1], s2_sb[:, c:c + 1], -1.0 / D, None,
                    ALU.mult)
                vts(tmpA[:, :], ssq2_sb[:, c:c + 1], 1.0 / D, None, ALU.mult)
                vtt(tmpB[:, :], nm2_sb[:, c:c + 1], nm2_sb[:, c:c + 1],
                    ALU.mult)
                vtt(tmpC[:, :], tmpA[:, :], tmpB[:, :], ALU.subtract)
                vts(w2_sb[:, c:c + 1], tmpC[:, :], 1e-3, None, ALU.add,
                    inc=("sV", 1))
                w2r[c] = bump("sV")
                WAIT("scalar", "sV", w2r[c])
                sact(tmpS[:, :], w2_sb[:, c:c + 1], ACT.Ln)
                sact(rs02_sb[:, c:c + 1], tmpS[:, :], ACT.Exp, scale=-0.5,
                     inc=("sS", 1))
                r02[c] = bump("sS")
                WAIT("vector", "sS", r02[c])
                vtt(tmpA[:, :], rs02_sb[:, c:c + 1], rs02_sb[:, c:c + 1],
                    ALU.mult)
                vtt(tmpB[:, :], w2_sb[:, c:c + 1], tmpA[:, :], ALU.mult)
                vts(tmpC[:, :], tmpB[:, :], -0.5, 1.5, ALU.mult, ALU.add)
                vtt(rs2_sb[:, c:c + 1], rs02_sb[:, c:c + 1], tmpC[:, :],
                    ALU.mult)
                vtt(nmrs2_sb[:, c:c + 1], nm2_sb[:, c:c + 1],
                    rs2_sb[:, c:c + 1], ALU.mult, inc=("sV", 1))
                nrm2[c] = bump("sV")
            if 0 <= c3 < NTC:
                c = c3
                WAIT("scalar", "sV", nrm2[c])
                if c >= 1:
                    WAIT("scalar", "sG", gdone[c - 1])
                sact(t2_scr[:, :, :], y_scr[c % 3][:, :, :], ACT.Identity,
                     scale=rs2_sb[:, c:c + 1], bias=nmrs2_sb[:, c:c + 1],
                     inc=("sS", 1))
                t2r[c] = bump("sS")
            if c1 < NTC:
                c = c1
                WAIT("vector", "sT", ymm[c])
                WAIT("vector", "sS", xnr[c])
                if c >= 3:
                    WAIT("vector", "sS", t2r[c - 3])
                vtt(y_scr[c % 3][:, :, :], P[:, (c % 2) * 2:(c % 2) * 2 + 2, :],
                    xn_rec[c % 2][:, :, :], ALU.add, inc=("sV", 1))
                yused[c] = bump("sV")
            if 0 <= c3 < NTC:
                c = c3
                if c == 0:
                    WAIT("gpsimd", "sV", v_gt)
                WAIT("gpsimd", "sS", t2r[c])
                if c >= 1:
                    WAIT("gpsimd", "sV", o1used[c - 1])
                OP("gpsimd", lambda e, c0=c: e.tensor_tensor(
                    g_scr[:, :, :], t2_scr[:, :, :], gam_t[:, :], ALU.mult),
                    inc=("sG", 1))
                gdone[c] = bump("sG")
                WAIT("vector", "sG", gdone[c])
                WAIT("vector", "sT", fmm[c])
                if c >= 1:
                    WAIT("vector", "sG", outt_v[c - 1])
                vtt(o1_scr[:, :, :], P[:, 4 + (c % 2) * 2:6 + (c % 2) * 2, :],
                    g_scr[:, :, :], ALU.add, inc=("sV", 1))
                o1used[c] = bump("sV")
                WAIT("gpsimd", "sV", o1used[c])
                if c >= 2:
                    WAIT("gpsimd", *odma[c - 2])
                OP("gpsimd", lambda e, c0=c: e.tensor_tensor(
                    ot[c0 % 2][:, :, :], o1_scr[:, :, :],
                    xr[c0 % 4][:, :, :], ALU.add), inc=("sG", 1))
                outt_v[c] = bump("sG")
                odma[c] = dqO.issue(out_d[c * 128:(c + 1) * 128, :],
                                    ot[c % 2][:, :, :],
                                    waits=[("sG", outt_v[c])])
                if c + 4 < NTC:
                    xrin[c + 4] = dqX.issue(
                        xr[(c + 4) % 4][:, :, :],
                        x_d[(c + 4) * 128:(c + 5) * 128, :],
                        waits=[("sG", outt_v[c])])

    try:
        _phases()
    except _Stop:
        pass

    DBG = {"s1": s1_sb, "ssq1": ssq1_sb, "w1": w1_sb, "rs1": rs1_sb,
           "rs0": rs0_sb, "nm1": nm1_sb, "nmrs1": nmrs1_sb,
           "xn_tm": xn_tm, "xnT": xnT, "Weff": Weff, "Wpv": Wpv,
           "Wpq": Wpq, "Wpk": Wpk, "qs": qs_sb, "qks": qks_sb,
           "rqB": rqB_sb, "Sbd": Sbd, "v2T": v2T, "attnT": attnT,
           "bqp_row": bqp_row, "bkp_row": bkp_row, "bvp_col": bvp_col,
           "rqT": rqT, "s2": s2_sb, "ssq2": ssq2_sb, "rs2": rs2_sb,
           "ident_b": ident_b, "selp": selp, "faT": faT, "cfb_row": cfb_row,
           "gam_t": gam_t, "WT_w": WT_w, "F_w": F_w, "E_w": E_w}
    for name in dump:
        h = DBG[name]
        dbg_d = nc.dram_tensor("dbg_" + name, list(h.shape), h.dtype,
                               kind="ExternalOutput")
        sl = tuple(slice(None) for _ in h.shape)
        dqX.issue(dbg_d[sl], h[sl])

    dqO.drain()
    dqW.drain()
    dqR.drain()
    dqX.drain()
    barrier()

    global _LAST_CNT
    _LAST_CNT = dict(cnt)
    with nc.Block() as blk:
        @blk.tensor
        def _(e):
            for f in plan["tensor"]:
                f(e)

        @blk.vector
        def _(e):
            for f in plan["vector"]:
                f(e)

        @blk.scalar
        def _(e):
            for f in plan["scalar"]:
                f(e)

        @blk.sync
        def _(e):
            for f in plan["sync"]:
                f(e)

        @blk.gpsimd
        def _(e):
            for f in plan["gpsimd"]:
                f(e)

    es.close()
    return nc


_NC_CACHE = None


def kernel(**inputs):
    global _NC_CACHE

    def g(name):
        return np.ascontiguousarray(np.asarray(inputs[name], dtype=np.float32))

    x = g("L0_attn_x")
    common = {
        "Wq": g("L0_attn_Wq"), "bq": g("L0_attn_bq"),
        "Wk": g("L0_attn_Wk"), "bk": g("L0_attn_bk"),
        "Wv": g("L0_attn_Wv"), "bv": g("L0_attn_bv"),
        "Wq2": g("L0_attn_Wq2"), "bq2": g("L0_attn_bq2"),
        "Wk2": g("L0_attn_Wk2"), "bk2": g("L0_attn_bk2"),
        "Wv2": g("L0_attn_Wv2"), "bv2": g("L0_attn_bv2"),
        "Wo": g("L0_attn_Wo"), "bo": g("L0_attn_bo"),
        "gamma": g("L0_attn_gamma"), "beta": g("L0_attn_beta"),
        "fk": g("L0_ffn_kernel"), "fbias": g("L0_ffn_bias"),
        "fa": g("L0_ffn_a"), "fb": g("L0_ffn_b"),
    }
    if _NC_CACHE is None:
        _NC_CACHE = _build()
    nc = _NC_CACHE
    in_maps = [dict(common, x=np.ascontiguousarray(x[b])) for b in range(8)]
    res = run_bass_kernel_spmd(nc, in_maps, core_ids=list(range(8)))
    out = np.stack([res.results[b]["out"] for b in range(8)], axis=0)
    return out.astype(np.float32)


# version 18
# speedup vs baseline: 1.2195x; 1.2195x over previous
import numpy as np

import concourse.bass as bass
import concourse.mybir as mybir
import concourse.bass_isa as bass_isa
from concourse.bass_utils import run_bass_kernel_spmd

F32 = mybir.dt.float32
BF16 = mybir.dt.bfloat16
ALU = mybir.AluOpType
ACT = mybir.ActivationFunctionType
AXL = mybir.AxisListType
RED = bass_isa.ReduceOp

T = 2048
D = 1024
NKC = 8
NTC = 16
NPAIR = 8
ENGINES = ("tensor", "vector", "scalar", "sync", "gpsimd")


def _build(upto=99, _PH5_BIAS=True, _PH5_S=True, _PH5_ELU=True, _PH5_QKS=True, _PH5_MM=True, dump=()):
    nc = bass.Bass("TRN2", target_bir_lowering=False,
                   detect_race_conditions=False)

    x_d = nc.dram_tensor("x", [T, D], F32, kind="ExternalInput")
    out_d = nc.dram_tensor("out", [T, D], F32, kind="ExternalOutput")
    dd = {}
    for n in ["Wq", "Wk", "Wv", "Wq2", "Wk2", "Wv2", "Wo", "fk"]:
        dd[n] = nc.dram_tensor(n, [D, D], F32, kind="ExternalInput")
    for n in ["bq", "bk", "bv", "bq2", "bk2", "bv2", "bo", "gamma", "beta",
              "fbias"]:
        dd[n] = nc.dram_tensor(n, [D], F32, kind="ExternalInput")
    dd["fa"] = nc.dram_tensor("fa", [D, 16], F32, kind="ExternalInput")
    dd["fb"] = nc.dram_tensor("fb", [16, D], F32, kind="ExternalInput")

    def wr(n):
        return dd[n].rearrange("(c p) j -> p c j", p=128)

    def rowr(n):
        return dd[n].rearrange("(o j) -> o j", o=1)

    from contextlib import ExitStack
    es = ExitStack()

    A = nc.alloc_sbuf_tensor
    ident_b = A("ident_b", [128, 128], BF16)
    ident_f = A("ident_f", [128, 128], F32)
    ones_col = A("ones_col", [1, 128], BF16)
    ones_f = A("ones_f", [1, 1], F32)
    xnT = A("xnT", [128, NKC, T], BF16)
    Wo_sb = A("Wo_sb", [128, NKC, D], BF16)
    Weff = A("Weff", [128, NKC, D], BF16)
    Wpv = A("Wpv", [128, NKC, D], BF16)
    s1_sb = A("s1_sb", [128, NTC], F32)
    ssq1_sb = A("ssq1_sb", [128, NTC], F32)
    nm1_sb = A("nm1_sb", [128, NTC], F32)
    w1_sb = A("w1_sb", [128, NTC], F32)
    rs1_sb = A("rs1_sb", [128, NTC], F32)
    nmrs1_sb = A("nmrs1_sb", [128, NTC], F32)
    rs0_sb = A("rs0_sb", [128, NTC], F32)
    s2_sb = A("s2_sb", [128, NTC], F32)
    ssq2_sb = A("ssq2_sb", [128, NTC], F32)
    nm2_sb = A("nm2_sb", [128, NTC], F32)
    w2_sb = A("w2_sb", [128, NTC], F32)
    rs2_sb = A("rs2_sb", [128, NTC], F32)
    nmrs2_sb = A("nmrs2_sb", [128, NTC], F32)
    rs02_sb = A("rs02_sb", [128, NTC], F32)
    tmpA = A("tmpA", [128, 1], F32)
    tmpB = A("tmpB", [128, 1], F32)
    tmpC = A("tmpC", [128, 1], F32)
    tmpS = A("tmpS", [128, 1], F32)
    qs_sb = A("qs_sb", [128, 8], F32)
    qs4_sb = A("qs4_sb", [128, 4], F32)
    qks_sb = A("qks_sb", [128, NTC * 16], F32)
    rqA_sb = A("rqA_sb", [128, NTC * 16], F32)
    rqB_sb = A("rqB_sb", [128, NTC * 16], F32)
    Sbd = A("Sbd", [128, NPAIR, 128], BF16)
    bqp_row = A("bqp_row", [1, D], BF16)
    bkp_row = A("bkp_row", [1, D], BF16)
    bo_row = A("bo_row", [1, D], BF16)
    cfb_row = A("cfb_row", [1, D], BF16)
    gamma_row = A("gamma_row", [1, D], F32)
    bvp_col = A("bvp_col", [128, 8], F32)
    bcol_st = A("bcol_st", [128, 8], BF16)
    fa_sb = A("fa_sb", [128, NKC, 16], BF16)
    faT = A("faT", [16, D], BF16)
    fb_sb = A("fb_sb", [16, D], BF16)
    selp = A("selp", [16, NPAIR, 2, 64], BF16)
    ones_row_f = A("ones_row_f", [1, 128], F32)
    qrow = A("qrow", [1, 128], F32)
    bar_s = A("bar_s", [1, 1], F32)
    bar_g = A("bar_g", [1, 1], F32)

    ARENA = 98304
    arena = A("arena", [128, ARENA], mybir.dt.uint8)
    arena_addr = nc.lookup_mloc(arena).addr

    def place(name, shape, dtype, off):
        h = nc.virtual_sbuf_tensor(name, list(shape), dtype)
        nc._tensor(h.name, h.shape, h.dtype, type="SB")
        mloc = nc.lookup_mloc(h)
        mloc.allocated = True
        mloc.addr = arena_addr + off
        return h

    xt = [place(f"xt{i}", [128, D], F32, i * 4096) for i in range(8)]
    xn_tm = place("xn_tm", [128, NTC, D], BF16, 32768)
    fbias_row = place("fbias_row", [1, D], F32, 65536)
    beta_row = place("beta_row", [1, D], F32, 69632)
    cfb_f32 = place("cfb_f32", [1, D], F32, 73728)
    garb_s = place("garb_s", [128, D], F32, 77824)
    garb_v = place("garb_v", [128, D], F32, 81920)
    fkh = [place(f"fkh{h}", [128, 4, D], F32, h * 16384) for h in range(2)]
    F_w = place("F_w", [128, NKC, D], BF16, 32768)
    E_w = place("E_w", [128, NKC, D], BF16, 49152)
    WT_w = place("WT_w", [128, NKC, D], BF16, 65536)
    stageA = place("stageA", [1, D], BF16, 81920)
    stageB = place("stageB", [1, D], BF16, 83968)
    bvp_row = place("bvp_row", [1, D], F32, 86016)
    Wpq = place("Wpq", [128, NKC, D], BF16, 0)
    Wpk = place("Wpk", [128, NKC, D], BF16, 16384)
    qk_pre = place("qk_pre", [128, 4, 512], BF16, 32768)
    m_scr = place("m_scr", [128, 2, 512], F32, 40960)
    m_scrR = place("m_scrR", [128, 16, 64], F32, 53248)
    e_scr = place("e_scr", [128, 2, 512], F32, 57344)
    q2c = [place(f"q2c{i}", [128, D], BF16, 45056 + i * 2048) for i in range(2)]
    k2c = [place(f"k2c{i}", [128, D], BF16, 49152 + i * 2048) for i in range(2)]
    v2T = place("v2T", [128, NKC, T], BF16, 0)
    rqT = place("rqT", [16, T], F32, 90112)
    attnT = place("attnT", [128, NKC, T], BF16, 32768)
    rqbc_sb = place("rqbc_sb", [128, T], F32, 65536)
    rqTb = place("rqTb", [16, T], BF16, 73728)
    xn_rec = [place("xn_rec0", [128, 2, 512], F32, 0),
              place("xn_rec1", [128, 2, 512], F32, 24576)]
    y_scr = [place("y_scr0", [128, 2, 512], F32, 4096),
             place("y_scr1", [128, 2, 512], F32, 28672),
             place("y_scr2", [128, 2, 512], F32, 94208)]
    t2_scr = place("t2_scr", [128, 2, 512], F32, 8192)
    g_scr = place("g_scr", [128, 2, 512], F32, 12288)
    o1_scr = place("o1_scr", [128, 2, 512], F32, 16384)
    gam_t = place("gam_t", [128, D], F32, 20480)
    sq_scr = place("sq_scr", [128, 2, 512], F32, 90112)
    qsrc = [place("qsrc0", [128, 1, D], F32, 65536),
            place("qsrc1", [128, 1, D], F32, 69632)]
    qA = place("qA", [128, 1, D], F32, 73728)
    qB = place("qB", [128, 1, D], F32, 77824)
    qC = place("qC", [128, 1, D], F32, 81920)
    m_k = place("m_k", [128, 2, 512], F32, 36864)
    e_k = place("e_k", [128, 2, 512], F32, 61440)
    r_q2 = place("r_q2", [128, 2, 512], F32, 86016)
    r_k2 = place("r_k2", [128, 2, 512], F32, 90112)
    xr = [place(f"xr{i}", [128, 2, 512], F32, 65536 + i * 4096) for i in range(4)]
    ot = [place(f"ot{i}", [128, 2, 512], F32, 81920 + i * 4096) for i in range(2)]

    P = es.enter_context(nc.psum_tensor("P", [128, 8, 512], F32))

    sem = {}
    semnames = ["sT", "sV", "sS", "sG", "sPH", "hzV", "hzS"]
    semnames += [f"dW{i}" for i in range(4)]
    semnames += [f"dX{i}" for i in range(4)]
    semnames += [f"dR{i}" for i in range(4)]
    semnames += [f"dO{i}" for i in range(2)]
    for n in semnames:
        sem[n] = es.enter_context(nc.semaphore(n))
    cnt = {n: 0 for n in sem}

    def bump(n, k=1):
        cnt[n] += k
        return cnt[n]

    plan = {e: [] for e in ENGINES}
    LAST = {}
    HIST = {"vector": [], "scalar": []}
    vpad = A("vpad", [128, 1], F32)
    spad = A("spad", [128, 1], F32)

    def _nm(ap):
        try:
            return ap.tensor.name
        except AttributeError:
            return None

    def _names(aps):
        out = set()
        for ap in aps:
            n = _nm(ap)
            if n and not n.startswith("P"):
                out.add(n)
        return out

    def _fence(eng):
        hzn = "hzV" if eng == "vector" else "hzS"
        if eng == "vector":
            plan[eng].append(
                lambda e, s=sem[hzn]: e.memset(vpad[:, :], 0.0).then_inc(s, 1))
        else:
            plan[eng].append(
                lambda e, s=sem[hzn]: e.activation(
                    spad[:, :], spad[:, :], ACT.Identity,
                    scale=0.0).then_inc(s, 1))
        v = bump(hzn)
        WAIT(eng, hzn, v)
        HIST[eng].clear()

    def OP(eng, fn, inc=None, reads=(), writes=()):
        if eng in HIST:
            rd = _names(reads)
            if rd and any(rd & h for h in HIST[eng]):
                _fence(eng)

        def g(e, eng=eng, fn=fn, inc=inc):
            ins = fn(e)
            LAST[eng] = ins
            if inc is not None:
                ins.then_inc(sem[inc[0]], inc[1])
        plan[eng].append(g)
        if eng in HIST:
            HIST[eng].append(_names(writes))

    def WAIT(eng, sname, val):
        if val <= 0:
            return
        plan[eng].append(lambda e, s=sem[sname], v=val: e.wait_ge(s, v))

    def barrier():
        t = bump("sPH", 4)
        tcount = cnt["sT"]
        OP("vector", lambda en: en.engine_nop(), inc=("sPH", 1))
        OP("scalar", lambda en: en.activation(
            bar_s[:, :], bar_s[:, :], ACT.Identity, scale=0.0),
            inc=("sPH", 1))
        OP("gpsimd", lambda en: en.memset(bar_g[:, :], 0.0), inc=("sPH", 1))
        OP("sync", lambda en: en.nop(), inc=("sPH", 1))
        for e in ENGINES:
            WAIT(e, "sPH", t)
            if tcount > 0 and e != "tensor":
                WAIT(e, "sT", tcount)
        return t

    class DQ:

        def __init__(self, prefix, n, eng):
            self.names = [f"{prefix}{i}" for i in range(n)]
            self.n = n
            self.i = 0
            self.eng = eng

        def issue(self, dst, src, waits=()):
            k = self.i % self.n
            self.i += 1
            sname = self.names[k]
            if cnt[sname] > 0:
                WAIT(self.eng, sname, cnt[sname])
            for sn, v in waits:
                WAIT(self.eng, sn, v)
            OP(self.eng, lambda e, d=dst, s=src: e.dma_start(out=d, in_=s),
               inc=(sname, 16))
            return (sname, bump(sname, 16))

        def drain(self):
            for sname in self.names:
                if cnt[sname] > 0:
                    WAIT(self.eng, sname, cnt[sname])

    dqW = DQ("dW", 4, "gpsimd")
    dqX = DQ("dX", 4, "sync")
    dqR = DQ("dR", 4, "scalar")
    dqO = DQ("dO", 2, "sync")

    def mm(out, lhsT, rhs, start, stop, inc=None, tr=False):
        OP("tensor",
           lambda e, o=out, l=lhsT, r=rhs, s1=start, s2=stop, t=tr:
           e.matmul(o, l, r, start=s1, stop=s2, is_transpose=(True if t else None)),
           inc=inc)

    def vtt(out, a, b, op, inc=None, eng="vector"):
        OP(eng, lambda e, o=out, x=a, y=b, p=op: e.tensor_tensor(o, x, y, p),
           inc=inc, reads=(a, b), writes=(out,))

    def vts(out, in0, s1, s2, op0, op1=ALU.bypass, inc=None, eng="vector"):
        rd = [in0] + [s for s in (s1, s2) if not isinstance(s, (int, float))
                      and s is not None]
        OP(eng,
           lambda e, o=out, i=in0, a=s1, b=s2, p0=op0, p1=op1:
           e.tensor_scalar(o, i, a, b, p0, p1), inc=inc,
           reads=tuple(rd), writes=(out,))

    def vstt(out, in0, sc, in1, op0, op1, accum=None, inc=None):
        rd = [in0, in1] + ([sc] if not isinstance(sc, (int, float)) else [])
        OP("vector",
           lambda e, o=out, i0=in0, s=sc, i1=in1, p0=op0, p1=op1, ac=accum:
           e.scalar_tensor_tensor(o, i0, s, i1, p0, p1, accum_out=ac),
           inc=inc, reads=tuple(rd), writes=(out, accum) if accum is not None
           else (out,))

    def vred(out, in_, op, absval=None, inc=None, axis=None):
        ax = axis if axis is not None else AXL.X
        OP("vector",
           lambda e, o=out, i=in_, p=op, av=absval, a=ax:
           e.tensor_reduce(o, i, a, p, apply_absolute_value=av), inc=inc,
           reads=(in_,), writes=(out,))

    def vcopy(out, in_, inc=None, eng="vector"):
        if eng == "scalar":
            OP(eng, lambda e, o=out, i=in_: e.copy(o, i), inc=inc,
               reads=(in_,), writes=(out,))
        else:
            OP(eng, lambda e, o=out, i=in_: e.tensor_copy(o, i), inc=inc,
               reads=(in_,), writes=(out,))

    def vrecip(out, in_, inc=None):
        OP("vector", lambda e, o=out, i=in_: e.reciprocal(o, i), inc=inc,
           reads=(in_,), writes=(out,))

    def sact(out, in_, func, bias=0.0, scale=1.0, accum=None, inc=None):
        rd = [in_] + [s for s in (bias, scale) if not isinstance(s, (int, float))]
        OP("scalar",
           lambda e, o=out, i=in_, f=func, b=bias, s=scale, ac=accum:
           e.activation(o, i, f, bias=b, scale=s, accum_out=ac), inc=inc,
           reads=tuple(rd), writes=(out,))

    def scopy(out, in_, inc=None):
        OP("scalar", lambda e, o=out, i=in_: e.copy(o, i), inc=inc,
           reads=(in_,), writes=(out,))

    class _Stop(Exception):
        pass

    class _Stop(Exception):
        pass

    def _phases():
        OP("vector", lambda e: e.memset(vpad[:, :], 1.0))
        vtt(vpad[:, :], vpad[:, :], vpad[:, :], ALU.mult)
        vtt(vpad[:, :], vpad[:, :], vpad[:, :], ALU.add)
        OP("gpsimd", lambda e: e.memset(ident_b[:, :], 0.0))
        OP("gpsimd", lambda e: e.memset(ident_f[:, :], 0.0))
        OP("gpsimd", lambda e: e.memset(ones_col[:, :], 1.0))
        OP("gpsimd", lambda e: e.memset(ones_f[:, :], 1.0))
        OP("gpsimd", lambda e: e.memset(ones_row_f[:, :], 1.0))
        OP("gpsimd", lambda e: e.memset(selp[:, :, :, :], 0.0), inc=("sG", 1))
        g_ms = bump("sG")
        WAIT("gpsimd", "sG", g_ms)
        OP("gpsimd", lambda e: e.affine_select(
            out=ident_b[:, :], in_=ident_b[:, :], compare_op=ALU.not_equal,
            fill=1.0, base=0, pattern=[[-1, 128]], channel_multiplier=1),
            inc=("sG", 1))
        WAIT("gpsimd", "sG", bump("sG"))
        OP("gpsimd", lambda e: e.affine_select(
            out=ident_f[:, :], in_=ident_f[:, :], compare_op=ALU.not_equal,
            fill=1.0, base=0, pattern=[[-1, 128]], channel_multiplier=1),
            inc=("sG", 1))
        WAIT("gpsimd", "sG", bump("sG"))
        OP("gpsimd", lambda e: e.affine_select(
            out=selp[:, :, :, :], in_=selp[:, :, :, :], compare_op=ALU.not_equal,
            fill=1.0, base=0, pattern=[[-2, NPAIR], [-1, 2], [0, 64]],
            channel_multiplier=1), inc=("sG", 1))
        g_ident = bump("sG")

        wv = {}

        def gdma(name, dst, src, waits=()):
            wv[name] = dqW.issue(dst, src, waits)

        gdma("bo", bo_row[:, :], rowr("bo"))
        gdma("gamma", gamma_row[:, :], rowr("gamma"))
        gdma("fbias", fbias_row[:, :], rowr("fbias"))
        gdma("beta", beta_row[:, :], rowr("beta"))
        gdma("fa", fa_sb[:, :, :], dd["fa"].rearrange("(c p) r -> p c r", p=128))
        gdma("fb", fb_sb[:, :], dd["fb"][:, :])
        gdma("Wo", Wo_sb[:, :, :], wr("Wo"))

        WAIT("vector", *wv["fbias"])
        WAIT("vector", *wv["beta"])
        vtt(cfb_f32[:, :], fbias_row[:, :], beta_row[:, :], ALU.add)
        vcopy(cfb_row[:, :], cfb_f32[:, :])

        xin = {}
        xn_done = {}
        ssum = {}
        wready = {}
        rs0r = {}
        sqr = {}
        ssqred = {}
        tr2 = {}
        evac_rec = {}

        def emit_tr(c):
            bp = (c % 2) * 2
            if c == 0:
                WAIT("tensor", "sG", g_ident)
            WAIT("tensor", *xn_done[c])
            if c >= 2:
                WAIT("tensor", evac_rec[c - 2][0], evac_rec[c - 2][1])
            for half in range(2):
                for k in range(4):
                    kc = 4 * half + k
                    last = (half == 1 and k == 3)
                    mm(P[:, bp + half:bp + half + 1, k * 128:(k + 1) * 128],
                       xn_tm[:, c:c + 1, kc * 128:(kc + 1) * 128],
                       ident_b[:, :], start=(k == 0), stop=(k == 3),
                       inc=("sT", 1) if last else None)
            tr2[c] = bump("sT")

        def emit_evac(c):
            bp = (c % 2) * 2
            eng = "scalar" if c % 2 == 0 else "vector"
            WAIT(eng, "sT", tr2[c])
            vcopy(xnT[:, 0:4, c * 128:(c + 1) * 128], P[:, bp:bp + 1, :],
                  eng=eng)
            sn = "sS" if eng == "scalar" else "sV"
            vcopy(xnT[:, 4:8, c * 128:(c + 1) * 128], P[:, bp + 1:bp + 2, :],
                  eng=eng, inc=(sn, 1))
            evac_rec[c] = (sn, bump(sn))
        def st_sq(c):
            WAIT("scalar", *xin[c])
            if c >= 2:
                WAIT("scalar", "sV", ssqred[c - 2])
            sact([garb_s, garb_v][c % 2][:, :], xt[c % 8][:, :],
                 ACT.Square, inc=("sS", 1))
            sqr[c] = bump("sS")

        def st_red(c):
            WAIT("vector", *xin[c])
            WAIT("vector", "sS", sqr[c])
            vred(ssq1_sb[:, c:c + 1], [garb_s, garb_v][c % 2][:, :],
                 ALU.add, inc=("sV", 1))
            ssqred[c] = bump("sV")
            vred(s1_sb[:, c:c + 1], xt[c % 8][:, :], ALU.add)
            vts(nm1_sb[:, c:c + 1], s1_sb[:, c:c + 1], -1.0 / D, None, ALU.mult)
            vts(tmpA[:, :], ssq1_sb[:, c:c + 1], 1.0 / D, None, ALU.mult)
            vtt(tmpB[:, :], nm1_sb[:, c:c + 1], nm1_sb[:, c:c + 1], ALU.mult)
            vtt(tmpC[:, :], tmpA[:, :], tmpB[:, :], ALU.subtract)
            vts(w1_sb[:, c:c + 1], tmpC[:, :], 1e-5, None, ALU.add,
                inc=("sV", 1))
            wready[c] = bump("sV")

        def st_lnexp(c):
            WAIT("scalar", "sV", wready[c])
            sact(tmpS[:, :], w1_sb[:, c:c + 1], ACT.Ln)
            sact(rs0_sb[:, c:c + 1], tmpS[:, :], ACT.Exp, scale=-0.5,
                 inc=("sS", 1))
            rs0r[c] = bump("sS")

        def st_newton(c):
            WAIT("vector", "sS", rs0r[c])
            vtt(tmpA[:, :], rs0_sb[:, c:c + 1], rs0_sb[:, c:c + 1], ALU.mult)
            vtt(tmpB[:, :], w1_sb[:, c:c + 1], tmpA[:, :], ALU.mult)
            vts(tmpC[:, :], tmpB[:, :], -0.5, 1.5, ALU.mult, ALU.add)
            vtt(rs1_sb[:, c:c + 1], rs0_sb[:, c:c + 1], tmpC[:, :], ALU.mult)
            vtt(nmrs1_sb[:, c:c + 1], nm1_sb[:, c:c + 1], rs1_sb[:, c:c + 1],
                ALU.mult, inc=("sV", 1))
            nwd[c] = bump("sV")

        def st_norm(c):
            WAIT("scalar", "sV", nwd[c])
            sact(xn_tm[:, c:c + 1, :], xt[c % 8][:, :], ACT.Identity,
                 bias=nmrs1_sb[:, c:c + 1], scale=rs1_sb[:, c:c + 1],
                 inc=("sS", 1))
            xn_done[c] = ("sS", bump("sS"))
            emit_tr(c)
            if c >= 1:
                emit_evac(c - 1)

        nwd = {}
        for t in range(NTC + 4):
            if t < NTC and t % 4 == 0:
                for c in range(t, t + 4):
                    w = [xn_done[c - 8]] if c >= 8 else []
                    xin[c] = dqX.issue(xt[c % 8][:, :],
                                       x_d[c * 128:(c + 1) * 128, :], waits=w)
            if t < NTC:
                st_sq(t)
            if 0 <= t - 1 < NTC:
                st_red(t - 1)
            if 0 <= t - 2 < NTC:
                st_lnexp(t - 2)
            if 0 <= t - 3 < NTC:
                st_newton(t - 3)
            if 0 <= t - 4 < NTC:
                st_norm(t - 4)
        emit_evac(NTC - 1)

        if upto <= 1:
            barrier()
            raise _Stop()
        if upto <= 2:
            barrier()
            raise _Stop()

        for ic in range(8):
            gdma("F0_%d" % ic, F_w[:, ic:ic + 1, :], wr("Wq")[:, ic:ic + 1, :],
                 waits=[("sT", tr2[ic])])

        d_fk0 = dqX.issue(fkh[0][:, :, :], wr("fk")[:, 0:4, :],
                          waits=[xn_done[11]])
        d_fk1 = dqX.issue(fkh[1][:, :, :], wr("fk")[:, 4:8, :],
                          waits=[xn_done[15]])
        for h in range(2):
            WAIT("vector", *(d_fk0 if h == 0 else d_fk1))
            vred(qs4_sb[:, :], fkh[h][:, :, :], ALU.max, absval=True)
            vred(qs_sb[:, h:h + 1], qs4_sb[:, :], ALU.max)
        vtt(qs_sb[:, 2:3], qs_sb[:, 0:1], qs_sb[:, 1:2], ALU.max, inc=("sV", 1))
        v_qmax = bump("sV")
        WAIT("tensor", "sV", v_qmax)
        WAIT("tensor", "sG", g_ident)
        WAIT("tensor", evac_rec[14][0], evac_rec[14][1])
        WAIT("tensor", evac_rec[15][0], evac_rec[15][1])
        mm(P[0:1, 2:3, 0:128], qs_sb[:, 2:3], ident_f[:, :], start=True,
           stop=True, inc=("sT", 1))
        t_qt = bump("sT")
        WAIT("vector", "sT", t_qt)
        vcopy(qrow[0:1, :], P[0:1, 2:3, 0:128])
        vred(qs_sb[0:1, 3:4], qrow[0:1, :], ALU.max, inc=("sV", 1))
        v_qrow = bump("sV")
        WAIT("tensor", "sV", v_qrow)
        mm(P[:, 2:3, 0:1], ones_row_f[0:1, :], qs_sb[0:1, 3:4], start=True,
           stop=True, inc=("sT", 1))
        t_qb = bump("sT")
        WAIT("vector", "sT", t_qb)
        vcopy(qs_sb[:, 3:4], P[:, 2:3, 0:1])
        OP("vector", lambda e: e.memset(tmpA[:, :], 7.0))
        vrecip(tmpB[:, :], tmpA[:, :])
        vtt(qs_sb[:, 4:5], qs_sb[:, 3:4], tmpB[:, :], ALU.mult)
        vrecip(qs_sb[:, 5:6], qs_sb[:, 4:5])
        vts(qs_sb[:, 6:7], qs_sb[:, 4:5], 8.0, None, ALU.mult)

        for ic in range(8):
            gdma("E0_%d" % ic, E_w[:, ic:ic + 1, :],
                 wr("Wq2")[:, ic:ic + 1, :], waits=[("sT", tr2[8 + ic])])

        WAIT("tensor", *wv["fa"])
        WAIT("tensor", "sG", g_ident)
        fat_g = {}
        for c in range(8):
            bank = 6 + c // 4
            mm(P[0:16, bank:bank + 1, (c % 4) * 128:(c % 4) * 128 + 128],
               fa_sb[:, c:c + 1, :], ident_b[:, :],
               start=(c % 4 == 0), stop=(c % 4 == 3),
               inc=("sT", 1) if c % 4 == 3 else None)
            if c % 4 == 3:
                fat_g[c // 4] = bump("sT")
        for i in range(2):
            WAIT("vector", "sT", fat_g[i])
            vcopy(faT[0:16, i * 512:(i + 1) * 512], P[0:16, 6 + i:7 + i, :],
                  inc=("sV", 1))
            v_fat = bump("sV")

        gdma("stA0", stageA[:, :], rowr("bq"), waits=[("sV", ssqred[15])])
        gdma("stB0", stageB[:, :], rowr("bq2"))

        barrier()
        if upto <= 3:
            raise _Stop()

        WEIGHTS = [
            ("q", "Wq", "Wq2", "bq", "bq2", Wpq, ("row", bqp_row)),
            ("k", "Wk", "Wk2", "bk", "bk2", Wpk, ("row", bkp_row)),
            ("v", "Wv", "Wv2", "bv", "bv2", Wpv, ("col", bvp_col)),
        ]
        tr_done = {}
        col_done = {}
        bias_done = {}
        prod_done = {}
        prod_evac_done = {}
        for wi, (tag, w1n, w2n, b1n, b2n, Wp_dst, brow) in enumerate(WEIGHTS):
            if wi >= 1:
                gdma("F%da" % wi, F_w[:, 0:4, :], wr(w1n)[:, 0:4, :],
                     waits=[("sT", tr_done[wi - 1])])
                gdma("F%d" % wi, F_w[:, 4:8, :], wr(w1n)[:, 4:8, :],
                     waits=[("sT", tr_done[wi - 1])])
                gdma("E%da" % wi, E_w[:, 0:4, :], wr(w2n)[:, 0:4, :],
                     waits=[("sT", prod_done[wi - 1])])
                gdma("E%d" % wi, E_w[:, 4:8, :], wr(w2n)[:, 4:8, :],
                     waits=[("sT", prod_done[wi - 1])])
                gdma("stA%d" % wi, stageA[:, :], rowr(b1n),
                     waits=[("sT", col_done[wi - 1])])
                gdma("stB%d" % wi, stageB[:, :], rowr(b2n),
                     waits=[("sT", bias_done[wi - 1])])
            fk_ = "F%d" % wi if wi >= 1 else "F0"
            ek_ = "E%d" % wi if wi >= 1 else "E0"
            if wi == 0:
                for ic in range(4, 8):
                    WAIT("tensor", *wv["F0_%d" % ic])
            else:
                WAIT("tensor", *wv[fk_])
                WAIT("tensor", *wv[fk_ + "a"])
            if wi >= 1:
                WAIT("tensor", "sS", prod_evac_done[wi - 1])
            wt_ev = {}
            for kc in range(8):
                for ich in range(2):
                    g = kc * 2 + ich
                    bank = g % 2
                    if g >= 2:
                        WAIT("tensor", "sV", wt_ev[g - 2])
                    for k in range(4):
                        ic = ich * 4 + k
                        mm(P[:, bank:bank + 1, k * 128:(k + 1) * 128],
                           F_w[:, ic:ic + 1, kc * 128:(kc + 1) * 128],
                           ident_b[:, :], start=(k == 0), stop=(k == 3),
                           inc=("sT", 1) if k == 3 else None)
                    t_g = bump("sT")
                    WAIT("vector", "sT", t_g)
                    vcopy(WT_w[:, kc:kc + 1, ich * 512:(ich + 1) * 512],
                          P[:, bank:bank + 1, :], inc=("sV", 1))
                    wt_ev[g] = bump("sV")
            tr_done[wi] = cnt["sT"]
            WAIT("tensor", *wv["stA%d" % wi])
            for kcc in range(8):
                mm(P[:, 2:3, kcc:kcc + 1],
                   stageA[0:1, kcc * 128:(kcc + 1) * 128], ones_col[0:1, 0:1],
                   start=(kcc == 0), stop=(kcc == 7),
                   inc=("sT", 1) if kcc == 7 else None)
            col_done[wi] = bump("sT")
            WAIT("vector", "sT", col_done[wi])
            vcopy(bcol_st[:, 0:8], P[:, 2:3, 0:8], inc=("sV", 1))
            v_col = bump("sV")
            WAIT("tensor", "sV", v_col)
            if wi == 0:
                for ic in range(4, 8):
                    WAIT("tensor", *wv["E0_%d" % ic])
            else:
                WAIT("tensor", *wv[ek_])
                WAIT("tensor", *wv[ek_ + "a"])
            WAIT("tensor", *wv["stB%d" % wi])
            for jh in range(2):
                bank = 4 + jh
                for kcc in range(8):
                    mm(P[0:1, bank:bank + 1, :], bcol_st[:, kcc:kcc + 1],
                       E_w[:, kcc:kcc + 1, jh * 512:(jh + 1) * 512],
                       start=(kcc == 0), stop=False)
                mm(P[0:1, bank:bank + 1, :], ones_col[0:1, 0:1],
                   stageB[0:1, jh * 512:(jh + 1) * 512], start=False, stop=True,
                   inc=("sT", 1) if jh == 1 else None)
            bias_done[wi] = bump("sT")
            WAIT("scalar", "sT", bias_done[wi])
            if brow[0] == "row":
                for jh in range(2):
                    scopy(brow[1][0:1, jh * 512:(jh + 1) * 512],
                          P[0:1, 4 + jh:5 + jh, :],
                          inc=("sS", 1) if jh == 1 else None)
                s_brow = bump("sS")
            else:
                for jh in range(2):
                    scopy(bvp_row[0:1, jh * 512:(jh + 1) * 512],
                          P[0:1, 4 + jh:5 + jh, :],
                          inc=("sS", 1) if jh == 1 else None)
                s_brow = bump("sS")
                WAIT("tensor", "sS", s_brow)
                for kcc in range(8):
                    mm(P[:, 6:7, kcc:kcc + 1],
                       bvp_row[0:1, kcc * 128:(kcc + 1) * 128], ones_f[0:1, 0:1],
                       start=(kcc == 0), stop=(kcc == 7),
                       inc=("sT", 1) if kcc == 7 else None)
                t_bvp = bump("sT")
                WAIT("vector", "sT", t_bvp)
                vcopy(bvp_col[:, 0:8], P[:, 6:7, 0:8], inc=("sV", 1))
                bump("sV")
            WAIT("tensor", "sV", wt_ev[15])
            WAIT("tensor", "sS", cnt["sS"])
            if brow[0] == "col":
                WAIT("tensor", "sV", cnt["sV"])
            pe_v = {}
            for jh in range(2):
                if jh == 1:
                    WAIT("tensor", "sS", pe_v[0])
                for kc in range(8):
                    for ic in range(8):
                        mm(P[:, ic:ic + 1, :],
                           WT_w[:, kc:kc + 1, ic * 128:(ic + 1) * 128],
                           E_w[:, kc:kc + 1, jh * 512:(jh + 1) * 512],
                           start=(kc == 0), stop=(kc == 7),
                           inc=("sT", 1) if (kc == 7 and ic == 7) else None)
                t_pj = bump("sT")
                WAIT("scalar", "sT", t_pj)
                for ic in range(8):
                    scopy(Wp_dst[:, ic:ic + 1, jh * 512:(jh + 1) * 512],
                          P[:, ic:ic + 1, :], inc=("sS", 1) if ic == 7 else None)
                pe_v[jh] = bump("sS")
                if jh == 1:
                    prod_done[wi] = t_pj
                    prod_evac_done[wi] = pe_v[1]

        barrier()
        if upto <= 4:
            raise _Stop()

        q2pre_v = {}
        k2pre_v = {}
        evq = {}
        evk = {}
        q2cd = {}
        k2cd = {}
        qm_g = {}
        vq_done = {}
        S_done = {}
        GE = ALU.is_ge
        q_rinv = qs_sb[:, 5:6]
        q_scale = qs_sb[:, 4:5]
        q_s8 = qs_sb[:, 6:7]
        d_q8 = {}
        lq_done = {}
        for qi in range(2):
            d_q8[qi] = dqX.issue(qsrc[qi][:, :, :], wr("fk")[:, qi:qi + 1, :])

        def gtt2(out, a, b, op, inc=None):
            OP("gpsimd",
               lambda e, o=out, x=a, y=b, p=op: e.tensor_tensor(o, x, y, p),
               inc=inc)

        def quant_part(qi, half):
            wdst = Weff[:, qi:qi + 1, :]
            S = qsrc[qi % 2][:, :, :]
            if half == 0:
                WAIT("vector", *d_q8[qi])
                vts(qA[:, :, :], S, q_rinv, 8.5, ALU.mult, ALU.add)
                vts(qB[:, :, :], qA[:, :, :], 8.0, None, GE)
                vstt(qC[:, :, :], qB[:, :, :], -8.0, qA[:, :, :],
                     ALU.mult, ALU.add)
                vts(qB[:, :, :], qC[:, :, :], 4.0, None, GE)
                vstt(S, qB[:, :, :], -4.0, qC[:, :, :], ALU.mult, ALU.add)
                vts(qB[:, :, :], S, 2.0, None, GE)
            else:
                vstt(qC[:, :, :], qB[:, :, :], -2.0, S, ALU.mult, ALU.add)
                vts(qB[:, :, :], qC[:, :, :], 1.0, None, GE)
                vstt(S, qB[:, :, :], -1.0, qC[:, :, :], ALU.mult, ALU.add)
                vtt(qC[:, :, :], qA[:, :, :], S, ALU.subtract)
                vts(wdst, qC[:, :, :], q_scale, q_s8, ALU.mult,
                    ALU.subtract, inc=("sV", 1))
                lq_done[qi] = ("sV", bump("sV"))

        for c in range(NTC):
            if c >= 1:
                WAIT("tensor", "sS", evk[c - 1])
            for kc in range(8):
                lh = xnT[:, kc:kc + 1, c * 128:(c + 1) * 128]
                mm(P[:, 0:1, :], lh, Wpq[:, kc:kc + 1, 0:512],
                   start=(kc == 0), stop=False)
                mm(P[:, 1:2, :], lh, Wpq[:, kc:kc + 1, 512:1024],
                   start=(kc == 0), stop=False)
                mm(P[:, 2:3, :], lh, Wpk[:, kc:kc + 1, 0:512],
                   start=(kc == 0), stop=False)
                mm(P[:, 3:4, :], lh, Wpk[:, kc:kc + 1, 512:1024],
                   start=(kc == 0), stop=False)
            mm(P[:, 0:1, :], ones_col[0:1, :], bqp_row[0:1, 0:512],
               start=False, stop=True)
            mm(P[:, 1:2, :], ones_col[0:1, :], bqp_row[0:1, 512:1024],
               start=False, stop=True, inc=("sT", 1))
            q2pre_v[c] = bump("sT")
            mm(P[:, 2:3, :], ones_col[0:1, :], bkp_row[0:1, 0:512],
               start=False, stop=True)
            mm(P[:, 3:4, :], ones_col[0:1, :], bkp_row[0:1, 512:1024],
               start=False, stop=True, inc=("sT", 1))
            k2pre_v[c] = bump("sT")
            if c >= 1:
                cp = c - 1
                WAIT("tensor", "sG", q2cd[cp])
                WAIT("tensor", "sG", k2cd[cp])
                for p in range(8):
                    bank = 4 + p // 4
                    mm(P[:, bank:bank + 1, (p % 4) * 128:(p % 4) * 128 + 128],
                       k2c[cp % 2][:, p * 128:(p + 1) * 128],
                       q2c[cp % 2][:, p * 128:(p + 1) * 128],
                       start=(cp == 0 and p % 4 == 0), stop=False,
                       inc=("sT", 1) if p == 7 else None)
                S_done[cp] = bump("sT")
            WAIT("scalar", "sT", q2pre_v[c])
            scopy(qk_pre[:, 0:2, :], P[:, 0:2, :], inc=("sS", 1))
            evq[c] = bump("sS")
            WAIT("scalar", "sT", k2pre_v[c])
            scopy(qk_pre[:, 2:4, :], P[:, 2:4, :], inc=("sS", 1))
            evk[c] = bump("sS")
            if c >= 1:
                WAIT("scalar", "sG", k2cd[c - 1])
            sact(m_scr[:, :, :], qk_pre[:, 0:2, :], ACT.Relu, scale=-1.0)
            sact(m_k[:, :, :], qk_pre[:, 2:4, :], ACT.Relu, scale=-1.0)
            sact(r_q2[:, :, :], qk_pre[:, 0:2, :], ACT.Relu)
            sact(r_k2[:, :, :], qk_pre[:, 2:4, :], ACT.Relu)
            sact(e_scr[:, :, :], m_scr[:, :, :], ACT.Exp, scale=-1.0,
                 inc=("sS", 1))
            s_eq = bump("sS")
            sact(e_k[:, :, :], m_k[:, :, :], ACT.Exp, scale=-1.0,
                 inc=("sS", 1))
            s_ek = bump("sS")
            if c >= 2:
                WAIT("gpsimd", "sT", S_done[c - 2])
            WAIT("gpsimd", "sS", s_eq)
            gtt2(q2c[c % 2][:, :], e_scr[:, :, :], r_q2[:, :, :], ALU.add,
                 inc=("sG", 1))
            q2cd[c] = bump("sG")
            WAIT("gpsimd", "sS", s_ek)
            gtt2(k2c[c % 2][:, :], e_k[:, :, :], r_k2[:, :, :], ALU.add,
                 inc=("sG", 1))
            k2cd[c] = bump("sG")
            if c >= 1:
                WAIT("gpsimd", "sV", vq_done[c - 1])
            WAIT("gpsimd", "sG", k2cd[c])
            gtt2(m_scrR[:, :, :], q2c[c % 2][:, :], k2c[c % 2][:, :],
                 ALU.mult, inc=("sG", 1))
            qm_g[c] = bump("sG")
            WAIT("vector", "sG", qm_g[c])
            vred(qks_sb[:, c * 16:(c + 1) * 16], m_scrR[:, :, :], ALU.add,
                 inc=("sV", 1))
            vq_done[c] = bump("sV")
            quant_part(c // 2, c % 2)
            if c % 2 == 1 and c < 12:
                qn = c // 2 + 2
                d_q8[qn] = dqX.issue(qsrc[qn % 2][:, :, :],
                                     wr("fk")[:, qn:qn + 1, :],
                                     waits=[lq_done[c // 2]])
        cp = NTC - 1
        WAIT("tensor", "sG", q2cd[cp])
        WAIT("tensor", "sG", k2cd[cp])
        for p in range(8):
            bank = 4 + p // 4
            mm(P[:, bank:bank + 1, (p % 4) * 128:(p % 4) * 128 + 128],
               k2c[cp % 2][:, p * 128:(p + 1) * 128],
               q2c[cp % 2][:, p * 128:(p + 1) * 128],
               start=False, stop=(p % 4 == 3),
               inc=("sT", 1) if p == 7 else None)
        S_done[cp] = bump("sT")
        WAIT("tensor", "sV", v_fat)
        WAIT("tensor", *wv["fb"])
        WAIT("tensor", *lq_done[7])
        wfev = {}
        for ic in range(8):
            for jf in range(2):
                g = ic * 2 + jf
                bank = 6 + g % 2
                if g >= 2:
                    WAIT("tensor", "sS", wfev[g - 2])
                mm(P[:, bank:bank + 1, :], faT[0:16, ic * 128:(ic + 1) * 128],
                   fb_sb[0:16, jf * 512:(jf + 1) * 512], start=True, stop=False)
                mm(P[:, bank:bank + 1, :], ident_b[:, :],
                   Weff[:, ic:ic + 1, jf * 512:(jf + 1) * 512],
                   start=False, stop=True, inc=("sT", 1))
                t_wf = bump("sT")
                WAIT("scalar", "sT", t_wf)
                scopy(Weff[:, ic:ic + 1, jf * 512:(jf + 1) * 512],
                      P[:, bank:bank + 1, :], inc=("sS", 1))
                wfev[g] = bump("sS")
        vts(rqA_sb[:, :], qks_sb[:, :], 1e-6, None, ALU.add)
        vrecip(rqB_sb[:, :], rqA_sb[:, :])
        OP("vector", lambda e: e.engine_nop(), inc=("sV", 1))
        v_rq = bump("sV")

        barrier()
        if upto <= 5:
            raise _Stop()

        rqt_ev = {}
        WAIT("tensor", "sV", v_rq)
        for g in range(4):
            bank = 6 + g % 2
            if g >= 2:
                WAIT("tensor", "sV", rqt_ev[g - 2])
            for j in range(4):
                tc = g * 4 + j
                mm(P[0:16, bank:bank + 1, j * 128:(j + 1) * 128],
                   rqB_sb[:, tc * 16:(tc + 1) * 16], ident_f[:, :],
                   start=(j == 0), stop=(j == 3), tr=True,
                   inc=("sT", 1) if j == 3 else None)
            t_rg = bump("sT")
            WAIT("vector", "sT", t_rg)
            vcopy(rqT[0:16, g * 512:(g + 1) * 512], P[0:16, bank:bank + 1, :],
                  inc=("sV", 1))
            rqt_ev[g] = bump("sV")
        v2ev = {}
        for mc in range(8):
            for u in range(2):
                unit = mc * 2 + u
                bs = (unit % 2) * 2
                if unit >= 2:
                    WAIT("tensor", "sS", v2ev[unit - 2])
                for kc in range(8):
                    for t2 in range(2):
                        th = u * 2 + t2
                        mm(P[:, bs + t2:bs + t2 + 1, :],
                           Wpv[:, kc:kc + 1, mc * 128:(mc + 1) * 128],
                           xnT[:, kc:kc + 1, th * 512:(th + 1) * 512],
                           start=(kc == 0), stop=(kc == 7),
                           inc=("sT", 1) if (kc == 7 and t2 == 1) else None)
                t_u = bump("sT")
                WAIT("scalar", "sT", t_u)
                for t2 in range(2):
                    th = u * 2 + t2
                    sact(v2T[:, mc:mc + 1, th * 512:(th + 1) * 512],
                         P[:, bs + t2:bs + t2 + 1, :], ACT.Identity,
                         bias=bvp_col[:, mc:mc + 1],
                         inc=("sS", 1) if t2 == 1 else None)
                v2ev[unit] = bump("sS")

        barrier()
        if upto <= 6:
            raise _Stop()

        OP("vector", lambda e: e.memset(Sbd[:, :, :], 0.0))
        for p in range(8):
            bank = 4 + p // 4
            off = (p % 4) * 128
            vcopy(Sbd[0:64, p:p + 1, 0:64], P[0:64, bank:bank + 1, off:off + 64])
            vcopy(Sbd[64:128, p:p + 1, 64:128],
                  P[64:128, bank:bank + 1, off + 64:off + 128],
                  inc=("sV", 1) if p == 7 else None)
        v_sbd = bump("sV")
        WAIT("vector", "sV", rqt_ev[3])
        vcopy(rqTb[0:16, :], rqT[0:16, :], inc=("sV", 1))
        v_rqtb = bump("sV")
        norm = {}
        rqev = {}
        for p in range(8):
            if p == 0:
                WAIT("tensor", "sV", v_sbd)
                WAIT("tensor", "sV", v_rqtb)
            if p >= 1:
                WAIT("tensor", "sS", rqev[p - 1])
            for th in range(4):
                mm(P[:, 4 + th:5 + th, :], selp[0:16, p:p + 1, :, :],
                   rqTb[0:16, th * 512:(th + 1) * 512], start=True, stop=True,
                   inc=("sT", 1) if th == 3 else None)
            t_rb = bump("sT")
            WAIT("scalar", "sT", t_rb)
            if p >= 1:
                WAIT("scalar", "sV", norm[p - 1])
            for th in range(4):
                scopy(rqbc_sb[:, th * 512:(th + 1) * 512], P[:, 4 + th:5 + th, :],
                      inc=("sS", 1) if th == 3 else None)
            rqev[p] = bump("sS")
            if p >= 1:
                WAIT("tensor", "sV", norm[p - 1])
            for th in range(4):
                mm(P[:, th:th + 1, :], Sbd[:, p:p + 1, :],
                   v2T[:, p:p + 1, th * 512:(th + 1) * 512], start=True, stop=True,
                   inc=("sT", 1) if th == 3 else None)
            t_o = bump("sT")
            WAIT("vector", "sT", t_o)
            WAIT("vector", "sS", rqev[p])
            for th in range(4):
                vtt(attnT[:, p:p + 1, th * 512:(th + 1) * 512],
                    P[:, th:th + 1, :],
                    rqbc_sb[:, th * 512:(th + 1) * 512], ALU.mult,
                    inc=("sV", 1) if th == 3 else None)
            norm[p] = bump("sV")

        barrier()
        if upto <= 7:
            raise _Stop()

        WAIT("tensor", *wv["gamma"])
        for jh in range(2):
            mm(P[:, 6 + jh:7 + jh, :], ones_row_f[0:1, :],
               gamma_row[0:1, jh * 512:(jh + 1) * 512], start=True, stop=True,
               inc=("sT", 1) if jh == 1 else None)
        t_gt = bump("sT")
        WAIT("vector", "sT", t_gt)
        vcopy(gam_t[:, 0:512], P[:, 6:7, :])
        vcopy(gam_t[:, 512:1024], P[:, 7:8, :], inc=("sV", 1))
        v_gt = bump("sV")

        xrin = {}
        for i in range(4):
            xrin[i] = dqR.issue(xr[i][:, :, :], x_d[i * 128:(i + 1) * 128, :])

        ymm = {}
        fmm = {}
        xnr = {}
        yused = {}
        o1used = {}
        outt_v = {}
        s2r = {}
        w2r = {}
        r02 = {}
        nrm2 = {}
        t2r = {}
        gdone = {}
        odma = {}
        for s in range(NTC + 2):
            c1 = s
            c2 = s - 1
            c3 = s - 2
            if c1 < NTC:
                c = c1
                b0 = (c % 2) * 2
                if c >= 2:
                    WAIT("tensor", "sV", yused[c - 2])
                for jh in range(2):
                    for kc in range(8):
                        mm(P[:, b0 + jh:b0 + jh + 1, :],
                           attnT[:, kc:kc + 1, c * 128:(c + 1) * 128],
                           Wo_sb[:, kc:kc + 1, jh * 512:(jh + 1) * 512],
                           start=(kc == 0), stop=False)
                    mm(P[:, b0 + jh:b0 + jh + 1, :], ones_col[0:1, :],
                       bo_row[0:1, jh * 512:(jh + 1) * 512], start=False,
                       stop=True, inc=("sT", 1) if jh == 1 else None)
                ymm[c] = bump("sT")
            if 0 <= c2 < NTC:
                c = c2
                b0 = (c % 2) * 2
                if c >= 2:
                    WAIT("tensor", "sV", o1used[c - 2])
                if c == 1:
                    WAIT("tensor", "sV", v_gt)
                for jh in range(2):
                    for kc in range(8):
                        mm(P[:, 4 + b0 + jh:5 + b0 + jh, :],
                           xnT[:, kc:kc + 1, c * 128:(c + 1) * 128],
                           Weff[:, kc:kc + 1, jh * 512:(jh + 1) * 512],
                           start=(kc == 0), stop=False)
                    mm(P[:, 4 + b0 + jh:5 + b0 + jh, :], ones_col[0:1, :],
                       cfb_row[0:1, jh * 512:(jh + 1) * 512], start=False,
                       stop=True, inc=("sT", 1) if jh == 1 else None)
                fmm[c] = bump("sT")
            if c1 < NTC:
                c = c1
                WAIT("scalar", *xrin[c])
                if c >= 2:
                    WAIT("scalar", "sV", yused[c - 2])
                sact(xn_rec[c % 2][:, :, :], xr[c % 4][:, :, :], ACT.Identity,
                     bias=nmrs1_sb[:, c:c + 1], scale=rs1_sb[:, c:c + 1],
                     inc=("sS", 1))
                xnr[c] = bump("sS")
            if 0 <= c2 < NTC:
                c = c2
                WAIT("scalar", "sV", yused[c])
                sact(sq_scr[:, :, :], y_scr[c % 3][:, :, :], ACT.Square,
                     inc=("sS", 1))
                s2r[c] = bump("sS")
                WAIT("vector", "sS", s2r[c])
                vred(qs4_sb[:, 0:2], y_scr[c % 3][:, :, :], ALU.add)
                vred(s2_sb[:, c:c + 1], qs4_sb[:, 0:2], ALU.add)
                vred(qs4_sb[:, 2:4], sq_scr[:, :, :], ALU.add)
                vred(ssq2_sb[:, c:c + 1], qs4_sb[:, 2:4], ALU.add)
                vts(nm2_sb[:, c:c + 1], s2_sb[:, c:c + 1], -1.0 / D, None,
                    ALU.mult)
                vts(tmpA[:, :], ssq2_sb[:, c:c + 1], 1.0 / D, None, ALU.mult)
                vtt(tmpB[:, :], nm2_sb[:, c:c + 1], nm2_sb[:, c:c + 1],
                    ALU.mult)
                vtt(tmpC[:, :], tmpA[:, :], tmpB[:, :], ALU.subtract)
                vts(w2_sb[:, c:c + 1], tmpC[:, :], 1e-3, None, ALU.add,
                    inc=("sV", 1))
                w2r[c] = bump("sV")
                WAIT("scalar", "sV", w2r[c])
                sact(tmpS[:, :], w2_sb[:, c:c + 1], ACT.Ln)
                sact(rs02_sb[:, c:c + 1], tmpS[:, :], ACT.Exp, scale=-0.5,
                     inc=("sS", 1))
                r02[c] = bump("sS")
                WAIT("vector", "sS", r02[c])
                vtt(tmpA[:, :], rs02_sb[:, c:c + 1], rs02_sb[:, c:c + 1],
                    ALU.mult)
                vtt(tmpB[:, :], w2_sb[:, c:c + 1], tmpA[:, :], ALU.mult)
                vts(tmpC[:, :], tmpB[:, :], -0.5, 1.5, ALU.mult, ALU.add)
                vtt(rs2_sb[:, c:c + 1], rs02_sb[:, c:c + 1], tmpC[:, :],
                    ALU.mult)
                vtt(nmrs2_sb[:, c:c + 1], nm2_sb[:, c:c + 1],
                    rs2_sb[:, c:c + 1], ALU.mult, inc=("sV", 1))
                nrm2[c] = bump("sV")
            if 0 <= c3 < NTC:
                c = c3
                WAIT("scalar", "sV", nrm2[c])
                if c >= 1:
                    WAIT("scalar", "sG", gdone[c - 1])
                sact(t2_scr[:, :, :], y_scr[c % 3][:, :, :], ACT.Identity,
                     scale=rs2_sb[:, c:c + 1], bias=nmrs2_sb[:, c:c + 1],
                     inc=("sS", 1))
                t2r[c] = bump("sS")
            if c1 < NTC:
                c = c1
                WAIT("vector", "sT", ymm[c])
                WAIT("vector", "sS", xnr[c])
                if c >= 3:
                    WAIT("vector", "sS", t2r[c - 3])
                vtt(y_scr[c % 3][:, :, :], P[:, (c % 2) * 2:(c % 2) * 2 + 2, :],
                    xn_rec[c % 2][:, :, :], ALU.add, inc=("sV", 1))
                yused[c] = bump("sV")
            if 0 <= c3 < NTC:
                c = c3
                if c == 0:
                    WAIT("gpsimd", "sV", v_gt)
                WAIT("gpsimd", "sS", t2r[c])
                if c >= 1:
                    WAIT("gpsimd", "sV", o1used[c - 1])
                OP("gpsimd", lambda e, c0=c: e.tensor_tensor(
                    g_scr[:, :, :], t2_scr[:, :, :], gam_t[:, :], ALU.mult),
                    inc=("sG", 1))
                gdone[c] = bump("sG")
                WAIT("vector", "sG", gdone[c])
                WAIT("vector", "sT", fmm[c])
                if c >= 1:
                    WAIT("vector", "sG", outt_v[c - 1])
                vtt(o1_scr[:, :, :], P[:, 4 + (c % 2) * 2:6 + (c % 2) * 2, :],
                    g_scr[:, :, :], ALU.add, inc=("sV", 1))
                o1used[c] = bump("sV")
                WAIT("gpsimd", "sV", o1used[c])
                if c >= 2:
                    WAIT("gpsimd", *odma[c - 2])
                OP("gpsimd", lambda e, c0=c: e.tensor_tensor(
                    ot[c0 % 2][:, :, :], o1_scr[:, :, :],
                    xr[c0 % 4][:, :, :], ALU.add), inc=("sG", 1))
                outt_v[c] = bump("sG")
                odma[c] = dqO.issue(out_d[c * 128:(c + 1) * 128, :],
                                    ot[c % 2][:, :, :],
                                    waits=[("sG", outt_v[c])])
                if c + 4 < NTC:
                    xrin[c + 4] = dqX.issue(
                        xr[(c + 4) % 4][:, :, :],
                        x_d[(c + 4) * 128:(c + 5) * 128, :],
                        waits=[("sG", outt_v[c])])

    try:
        _phases()
    except _Stop:
        pass

    DBG = {"s1": s1_sb, "ssq1": ssq1_sb, "w1": w1_sb, "rs1": rs1_sb,
           "rs0": rs0_sb, "nm1": nm1_sb, "nmrs1": nmrs1_sb,
           "xn_tm": xn_tm, "xnT": xnT, "Weff": Weff, "Wpv": Wpv,
           "Wpq": Wpq, "Wpk": Wpk, "qs": qs_sb, "qks": qks_sb,
           "rqB": rqB_sb, "Sbd": Sbd, "v2T": v2T, "attnT": attnT,
           "bqp_row": bqp_row, "bkp_row": bkp_row, "bvp_col": bvp_col,
           "rqT": rqT, "s2": s2_sb, "ssq2": ssq2_sb, "rs2": rs2_sb,
           "ident_b": ident_b, "selp": selp, "faT": faT, "cfb_row": cfb_row,
           "gam_t": gam_t, "WT_w": WT_w, "F_w": F_w, "E_w": E_w}
    for name in dump:
        h = DBG[name]
        dbg_d = nc.dram_tensor("dbg_" + name, list(h.shape), h.dtype,
                               kind="ExternalOutput")
        sl = tuple(slice(None) for _ in h.shape)
        dqX.issue(dbg_d[sl], h[sl])

    dqO.drain()
    dqW.drain()
    dqR.drain()
    dqX.drain()
    barrier()

    global _LAST_CNT
    _LAST_CNT = dict(cnt)
    with nc.Block() as blk:
        @blk.tensor
        def _(e):
            for f in plan["tensor"]:
                f(e)

        @blk.vector
        def _(e):
            for f in plan["vector"]:
                f(e)

        @blk.scalar
        def _(e):
            for f in plan["scalar"]:
                f(e)

        @blk.sync
        def _(e):
            for f in plan["sync"]:
                f(e)

        @blk.gpsimd
        def _(e):
            for f in plan["gpsimd"]:
                f(e)

    es.close()
    return nc


_NC_CACHE = None


def kernel(**inputs):
    global _NC_CACHE

    def g(name):
        return np.ascontiguousarray(np.asarray(inputs[name], dtype=np.float32))

    x = g("L0_attn_x")
    common = {
        "Wq": g("L0_attn_Wq"), "bq": g("L0_attn_bq"),
        "Wk": g("L0_attn_Wk"), "bk": g("L0_attn_bk"),
        "Wv": g("L0_attn_Wv"), "bv": g("L0_attn_bv"),
        "Wq2": g("L0_attn_Wq2"), "bq2": g("L0_attn_bq2"),
        "Wk2": g("L0_attn_Wk2"), "bk2": g("L0_attn_bk2"),
        "Wv2": g("L0_attn_Wv2"), "bv2": g("L0_attn_bv2"),
        "Wo": g("L0_attn_Wo"), "bo": g("L0_attn_bo"),
        "gamma": g("L0_attn_gamma"), "beta": g("L0_attn_beta"),
        "fk": g("L0_ffn_kernel"), "fbias": g("L0_ffn_bias"),
        "fa": g("L0_ffn_a"), "fb": g("L0_ffn_b"),
    }
    if _NC_CACHE is None:
        _NC_CACHE = _build()
    nc = _NC_CACHE
    in_maps = [dict(common, x=np.ascontiguousarray(x[b])) for b in range(8)]
    res = run_bass_kernel_spmd(nc, in_maps, core_ids=list(range(8)))
    out = np.stack([res.results[b]["out"] for b in range(8)], axis=0)
    return out.astype(np.float32)


# version 19
# speedup vs baseline: 1.2244x; 1.0040x over previous
import numpy as np

import concourse.bass as bass
import concourse.mybir as mybir
import concourse.bass_isa as bass_isa
from concourse.bass_utils import run_bass_kernel_spmd

F32 = mybir.dt.float32
BF16 = mybir.dt.bfloat16
ALU = mybir.AluOpType
ACT = mybir.ActivationFunctionType
AXL = mybir.AxisListType
RED = bass_isa.ReduceOp

T = 2048
D = 1024
NKC = 8
NTC = 16
NPAIR = 8
ENGINES = ("tensor", "vector", "scalar", "sync", "gpsimd")


def _build(upto=99, _PH5_BIAS=True, _PH5_S=True, _PH5_ELU=True, _PH5_QKS=True, _PH5_MM=True, dump=()):
    nc = bass.Bass("TRN2", target_bir_lowering=False,
                   detect_race_conditions=False)

    x_d = nc.dram_tensor("x", [T, D], F32, kind="ExternalInput")
    out_d = nc.dram_tensor("out", [T, D], F32, kind="ExternalOutput")
    dd = {}
    for n in ["Wq", "Wk", "Wv", "Wq2", "Wk2", "Wv2", "Wo", "fk"]:
        dd[n] = nc.dram_tensor(n, [D, D], F32, kind="ExternalInput")
    for n in ["bq", "bk", "bv", "bq2", "bk2", "bv2", "bo", "gamma", "beta",
              "fbias"]:
        dd[n] = nc.dram_tensor(n, [D], F32, kind="ExternalInput")
    dd["fa"] = nc.dram_tensor("fa", [D, 16], F32, kind="ExternalInput")
    dd["fb"] = nc.dram_tensor("fb", [16, D], F32, kind="ExternalInput")

    def wr(n):
        return dd[n].rearrange("(c p) j -> p c j", p=128)

    def rowr(n):
        return dd[n].rearrange("(o j) -> o j", o=1)

    from contextlib import ExitStack
    es = ExitStack()

    A = nc.alloc_sbuf_tensor
    ident_b = A("ident_b", [128, 128], BF16)
    ident_f = A("ident_f", [128, 128], F32)
    ones_col = A("ones_col", [1, 128], BF16)
    ones_f = A("ones_f", [1, 1], F32)
    xnT = A("xnT", [128, NKC, T], BF16)
    Wo_sb = A("Wo_sb", [128, NKC, D], BF16)
    Weff = A("Weff", [128, NKC, D], BF16)
    Wpv = A("Wpv", [128, NKC, D], BF16)
    s1_sb = A("s1_sb", [128, NTC], F32)
    ssq1_sb = A("ssq1_sb", [128, NTC], F32)
    nm1_sb = A("nm1_sb", [128, NTC], F32)
    w1_sb = A("w1_sb", [128, NTC], F32)
    rs1_sb = A("rs1_sb", [128, NTC], F32)
    nmrs1_sb = A("nmrs1_sb", [128, NTC], F32)
    rs0_sb = A("rs0_sb", [128, NTC], F32)
    s2_sb = A("s2_sb", [128, NTC], F32)
    ssq2_sb = A("ssq2_sb", [128, NTC], F32)
    nm2_sb = A("nm2_sb", [128, NTC], F32)
    w2_sb = A("w2_sb", [128, NTC], F32)
    rs2_sb = A("rs2_sb", [128, NTC], F32)
    nmrs2_sb = A("nmrs2_sb", [128, NTC], F32)
    rs02_sb = A("rs02_sb", [128, NTC], F32)
    tmpA = A("tmpA", [128, 1], F32)
    tmpB = A("tmpB", [128, 1], F32)
    tmpC = A("tmpC", [128, 1], F32)
    tmpS = A("tmpS", [128, 1], F32)
    qs_sb = A("qs_sb", [128, 8], F32)
    qs4_sb = A("qs4_sb", [128, 4], F32)
    qks_sb = A("qks_sb", [128, NTC * 16], F32)
    rqA_sb = A("rqA_sb", [128, NTC * 16], F32)
    rqB_sb = A("rqB_sb", [128, NTC * 16], F32)
    Sbd = A("Sbd", [128, NPAIR, 128], BF16)
    bqp_row = A("bqp_row", [1, D], BF16)
    bkp_row = A("bkp_row", [1, D], BF16)
    bo_row = A("bo_row", [1, D], BF16)
    cfb_row = A("cfb_row", [1, D], BF16)
    gamma_row = A("gamma_row", [1, D], F32)
    bvp_col = A("bvp_col", [128, 8], F32)
    bcol_st = A("bcol_st", [128, 8], BF16)
    fa_sb = A("fa_sb", [128, NKC, 16], BF16)
    faT = A("faT", [16, D], BF16)
    fb_sb = A("fb_sb", [16, D], BF16)
    selp = A("selp", [16, NPAIR, 2, 64], BF16)
    ones_row_f = A("ones_row_f", [1, 128], F32)
    qrow = A("qrow", [1, 128], F32)
    bar_s = A("bar_s", [1, 1], F32)
    bar_g = A("bar_g", [1, 1], F32)

    ARENA = 98304
    arena = A("arena", [128, ARENA], mybir.dt.uint8)
    arena_addr = nc.lookup_mloc(arena).addr

    def place(name, shape, dtype, off):
        h = nc.virtual_sbuf_tensor(name, list(shape), dtype)
        nc._tensor(h.name, h.shape, h.dtype, type="SB")
        mloc = nc.lookup_mloc(h)
        mloc.allocated = True
        mloc.addr = arena_addr + off
        return h

    xt = [place(f"xt{i}", [128, D], F32, i * 4096) for i in range(8)]
    xn_tm = place("xn_tm", [128, NTC, D], BF16, 32768)
    fbias_row = place("fbias_row", [1, D], F32, 65536)
    beta_row = place("beta_row", [1, D], F32, 69632)
    cfb_f32 = place("cfb_f32", [1, D], F32, 73728)
    garb_s = place("garb_s", [128, D], F32, 77824)
    garb_v = place("garb_v", [128, D], F32, 81920)
    fkh = [place(f"fkh{h}", [128, 4, D], F32, h * 16384) for h in range(2)]
    F_w = place("F_w", [128, NKC, D], BF16, 32768)
    E_w = place("E_w", [128, NKC, D], BF16, 49152)
    WT_w = place("WT_w", [128, NKC, D], BF16, 65536)
    stageA = place("stageA", [1, D], BF16, 81920)
    stageB = place("stageB", [1, D], BF16, 83968)
    bvp_row = place("bvp_row", [1, D], F32, 86016)
    Wpq = place("Wpq", [128, NKC, D], BF16, 0)
    Wpk = place("Wpk", [128, NKC, D], BF16, 16384)
    qk_pre = place("qk_pre", [128, 4, 512], BF16, 32768)
    m_scr = place("m_scr", [128, 2, 512], F32, 40960)
    m_scrR = place("m_scrR", [128, 16, 64], F32, 53248)
    e_scr = place("e_scr", [128, 2, 512], F32, 57344)
    q2c = [place(f"q2c{i}", [128, D], BF16, 45056 + i * 2048) for i in range(2)]
    k2c = [place(f"k2c{i}", [128, D], BF16, 49152 + i * 2048) for i in range(2)]
    v2T = place("v2T", [128, NKC, T], BF16, 0)
    rqT = place("rqT", [16, T], F32, 90112)
    attnT = place("attnT", [128, NKC, T], BF16, 32768)
    rqbc_sb = place("rqbc_sb", [128, T], F32, 65536)
    rqTb = place("rqTb", [16, T], BF16, 73728)
    xn_rec = [place("xn_rec0", [128, 2, 512], F32, 0),
              place("xn_rec1", [128, 2, 512], F32, 24576)]
    y_scr = [place("y_scr0", [128, 2, 512], F32, 4096),
             place("y_scr1", [128, 2, 512], F32, 28672),
             place("y_scr2", [128, 2, 512], F32, 94208)]
    t2_scr = place("t2_scr", [128, 2, 512], F32, 8192)
    g_scr = place("g_scr", [128, 2, 512], F32, 12288)
    o1_scr = place("o1_scr", [128, 2, 512], F32, 16384)
    gam_t = place("gam_t", [128, D], F32, 20480)
    sq_scr = place("sq_scr", [128, 2, 512], F32, 90112)
    qsrc = [place("qsrc0", [128, 1, D], F32, 65536),
            place("qsrc1", [128, 1, D], F32, 69632)]
    qA = place("qA", [128, 1, D], F32, 73728)
    qB = place("qB", [128, 1, D], F32, 77824)
    qC = place("qC", [128, 1, D], F32, 81920)
    m_k = place("m_k", [128, 2, 512], F32, 36864)
    e_k = place("e_k", [128, 2, 512], F32, 61440)
    r_q2 = place("r_q2", [128, 2, 512], F32, 86016)
    r_k2 = place("r_k2", [128, 2, 512], F32, 90112)
    xr = [place(f"xr{i}", [128, 2, 512], F32, 65536 + i * 4096) for i in range(4)]
    ot = [place(f"ot{i}", [128, 2, 512], F32, 81920 + i * 4096) for i in range(2)]

    P = es.enter_context(nc.psum_tensor("P", [128, 8, 512], F32))

    sem = {}
    semnames = ["sT", "sV", "sS", "sG", "sPH", "hzV", "hzS"]
    semnames += [f"dW{i}" for i in range(4)]
    semnames += [f"dX{i}" for i in range(4)]
    semnames += [f"dR{i}" for i in range(4)]
    semnames += [f"dO{i}" for i in range(2)]
    for n in semnames:
        sem[n] = es.enter_context(nc.semaphore(n))
    cnt = {n: 0 for n in sem}

    def bump(n, k=1):
        cnt[n] += k
        return cnt[n]

    plan = {e: [] for e in ENGINES}
    LAST = {}
    HIST = {"vector": [], "scalar": []}
    vpad = A("vpad", [128, 1], F32)
    spad = A("spad", [128, 1], F32)
    ptA = A("ptA", [128, 1], F32)
    ptB = A("ptB", [128, 1], F32)
    ptC = A("ptC", [128, 1], F32)

    def _nm(ap):
        try:
            return ap.tensor.name
        except AttributeError:
            return None

    def _names(aps):
        out = set()
        for ap in aps:
            n = _nm(ap)
            if n and not n.startswith("P"):
                out.add(n)
        return out

    def _fence(eng):
        hzn = "hzV" if eng == "vector" else "hzS"
        if eng == "vector":
            plan[eng].append(
                lambda e, s=sem[hzn]: e.memset(vpad[:, :], 0.0).then_inc(s, 1))
        else:
            plan[eng].append(
                lambda e, s=sem[hzn]: e.activation(
                    spad[:, :], spad[:, :], ACT.Identity,
                    scale=0.0).then_inc(s, 1))
        v = bump(hzn)
        WAIT(eng, hzn, v)
        HIST[eng].clear()

    def OP(eng, fn, inc=None, reads=(), writes=()):
        if eng in HIST:
            rd = _names(reads)
            if rd and any(rd & h for h in HIST[eng]):
                _fence(eng)

        def g(e, eng=eng, fn=fn, inc=inc):
            ins = fn(e)
            LAST[eng] = ins
            if inc is not None:
                ins.then_inc(sem[inc[0]], inc[1])
        plan[eng].append(g)
        if eng in HIST:
            HIST[eng].append(_names(writes))

    def WAIT(eng, sname, val):
        if val <= 0:
            return
        plan[eng].append(lambda e, s=sem[sname], v=val: e.wait_ge(s, v))

    def barrier():
        t = bump("sPH", 4)
        tcount = cnt["sT"]
        OP("vector", lambda en: en.engine_nop(), inc=("sPH", 1))
        OP("scalar", lambda en: en.activation(
            bar_s[:, :], bar_s[:, :], ACT.Identity, scale=0.0),
            inc=("sPH", 1))
        OP("gpsimd", lambda en: en.memset(bar_g[:, :], 0.0), inc=("sPH", 1))
        OP("sync", lambda en: en.nop(), inc=("sPH", 1))
        for e in ENGINES:
            WAIT(e, "sPH", t)
            if tcount > 0 and e != "tensor":
                WAIT(e, "sT", tcount)
        return t

    class DQ:

        def __init__(self, prefix, n, eng):
            self.names = [f"{prefix}{i}" for i in range(n)]
            self.n = n
            self.i = 0
            self.eng = eng

        def issue(self, dst, src, waits=()):
            k = self.i % self.n
            self.i += 1
            sname = self.names[k]
            if cnt[sname] > 0:
                WAIT(self.eng, sname, cnt[sname])
            for sn, v in waits:
                WAIT(self.eng, sn, v)
            OP(self.eng, lambda e, d=dst, s=src: e.dma_start(out=d, in_=s),
               inc=(sname, 16))
            return (sname, bump(sname, 16))

        def drain(self):
            for sname in self.names:
                if cnt[sname] > 0:
                    WAIT(self.eng, sname, cnt[sname])

    dqW = DQ("dW", 4, "gpsimd")
    dqX = DQ("dX", 4, "sync")
    dqR = DQ("dR", 4, "scalar")
    dqO = DQ("dO", 2, "sync")

    def mm(out, lhsT, rhs, start, stop, inc=None, tr=False):
        OP("tensor",
           lambda e, o=out, l=lhsT, r=rhs, s1=start, s2=stop, t=tr:
           e.matmul(o, l, r, start=s1, stop=s2, is_transpose=(True if t else None)),
           inc=inc)

    def vtt(out, a, b, op, inc=None, eng="vector"):
        OP(eng, lambda e, o=out, x=a, y=b, p=op: e.tensor_tensor(o, x, y, p),
           inc=inc, reads=(a, b), writes=(out,))

    def vts(out, in0, s1, s2, op0, op1=ALU.bypass, inc=None, eng="vector"):
        rd = [in0] + [s for s in (s1, s2) if not isinstance(s, (int, float))
                      and s is not None]
        OP(eng,
           lambda e, o=out, i=in0, a=s1, b=s2, p0=op0, p1=op1:
           e.tensor_scalar(o, i, a, b, p0, p1), inc=inc,
           reads=tuple(rd), writes=(out,))

    def vstt(out, in0, sc, in1, op0, op1, accum=None, inc=None):
        rd = [in0, in1] + ([sc] if not isinstance(sc, (int, float)) else [])
        OP("vector",
           lambda e, o=out, i0=in0, s=sc, i1=in1, p0=op0, p1=op1, ac=accum:
           e.scalar_tensor_tensor(o, i0, s, i1, p0, p1, accum_out=ac),
           inc=inc, reads=tuple(rd), writes=(out, accum) if accum is not None
           else (out,))

    def vred(out, in_, op, absval=None, inc=None, axis=None):
        ax = axis if axis is not None else AXL.X
        OP("vector",
           lambda e, o=out, i=in_, p=op, av=absval, a=ax:
           e.tensor_reduce(o, i, a, p, apply_absolute_value=av), inc=inc,
           reads=(in_,), writes=(out,))

    def vcopy(out, in_, inc=None, eng="vector"):
        if eng == "scalar":
            OP(eng, lambda e, o=out, i=in_: e.copy(o, i), inc=inc,
               reads=(in_,), writes=(out,))
        else:
            OP(eng, lambda e, o=out, i=in_: e.tensor_copy(o, i), inc=inc,
               reads=(in_,), writes=(out,))

    def vrecip(out, in_, inc=None):
        OP("vector", lambda e, o=out, i=in_: e.reciprocal(o, i), inc=inc,
           reads=(in_,), writes=(out,))

    def sact(out, in_, func, bias=0.0, scale=1.0, accum=None, inc=None):
        rd = [in_] + [s for s in (bias, scale) if not isinstance(s, (int, float))]
        OP("scalar",
           lambda e, o=out, i=in_, f=func, b=bias, s=scale, ac=accum:
           e.activation(o, i, f, bias=b, scale=s, accum_out=ac), inc=inc,
           reads=tuple(rd), writes=(out,))

    def scopy(out, in_, inc=None):
        OP("scalar", lambda e, o=out, i=in_: e.copy(o, i), inc=inc,
           reads=(in_,), writes=(out,))

    class _Stop(Exception):
        pass

    class _Stop(Exception):
        pass

    def _phases():
        OP("vector", lambda e: e.memset(vpad[:, :], 1.0))
        vtt(vpad[:, :], vpad[:, :], vpad[:, :], ALU.mult)
        vtt(vpad[:, :], vpad[:, :], vpad[:, :], ALU.add)
        OP("gpsimd", lambda e: e.memset(ident_b[:, :], 0.0))
        OP("gpsimd", lambda e: e.memset(ident_f[:, :], 0.0))
        OP("gpsimd", lambda e: e.memset(ones_col[:, :], 1.0))
        OP("gpsimd", lambda e: e.memset(ones_f[:, :], 1.0))
        OP("gpsimd", lambda e: e.memset(ones_row_f[:, :], 1.0))
        OP("gpsimd", lambda e: e.memset(selp[:, :, :, :], 0.0), inc=("sG", 1))
        g_ms = bump("sG")
        WAIT("gpsimd", "sG", g_ms)
        OP("gpsimd", lambda e: e.affine_select(
            out=ident_b[:, :], in_=ident_b[:, :], compare_op=ALU.not_equal,
            fill=1.0, base=0, pattern=[[-1, 128]], channel_multiplier=1),
            inc=("sG", 1))
        WAIT("gpsimd", "sG", bump("sG"))
        OP("gpsimd", lambda e: e.affine_select(
            out=ident_f[:, :], in_=ident_f[:, :], compare_op=ALU.not_equal,
            fill=1.0, base=0, pattern=[[-1, 128]], channel_multiplier=1),
            inc=("sG", 1))
        WAIT("gpsimd", "sG", bump("sG"))
        OP("gpsimd", lambda e: e.affine_select(
            out=selp[:, :, :, :], in_=selp[:, :, :, :], compare_op=ALU.not_equal,
            fill=1.0, base=0, pattern=[[-2, NPAIR], [-1, 2], [0, 64]],
            channel_multiplier=1), inc=("sG", 1))
        g_ident = bump("sG")

        wv = {}

        def gdma(name, dst, src, waits=()):
            wv[name] = dqW.issue(dst, src, waits)

        gdma("bo", bo_row[:, :], rowr("bo"))
        gdma("gamma", gamma_row[:, :], rowr("gamma"))
        gdma("fbias", fbias_row[:, :], rowr("fbias"))
        gdma("beta", beta_row[:, :], rowr("beta"))
        gdma("fa", fa_sb[:, :, :], dd["fa"].rearrange("(c p) r -> p c r", p=128))
        gdma("fb", fb_sb[:, :], dd["fb"][:, :])
        gdma("Wo", Wo_sb[:, :, :], wr("Wo"))

        WAIT("vector", *wv["fbias"])
        WAIT("vector", *wv["beta"])
        vtt(cfb_f32[:, :], fbias_row[:, :], beta_row[:, :], ALU.add)
        vcopy(cfb_row[:, :], cfb_f32[:, :])

        xin = {}
        xn_done = {}
        ssum = {}
        wready = {}
        rs0r = {}
        sqr = {}
        ssqred = {}
        tr2 = {}
        evac_rec = {}

        def emit_tr(c):
            bp = (c % 2) * 2
            if c == 0:
                WAIT("tensor", "sG", g_ident)
            WAIT("tensor", *xn_done[c])
            if c >= 2:
                WAIT("tensor", evac_rec[c - 2][0], evac_rec[c - 2][1])
            for half in range(2):
                for k in range(4):
                    kc = 4 * half + k
                    last = (half == 1 and k == 3)
                    mm(P[:, bp + half:bp + half + 1, k * 128:(k + 1) * 128],
                       xn_tm[:, c:c + 1, kc * 128:(kc + 1) * 128],
                       ident_b[:, :], start=(k == 0), stop=(k == 3),
                       inc=("sT", 1) if last else None)
            tr2[c] = bump("sT")

        def emit_evac(c):
            bp = (c % 2) * 2
            eng = "scalar" if c % 2 == 0 else "vector"
            WAIT(eng, "sT", tr2[c])
            vcopy(xnT[:, 0:4, c * 128:(c + 1) * 128], P[:, bp:bp + 1, :],
                  eng=eng)
            sn = "sS" if eng == "scalar" else "sV"
            vcopy(xnT[:, 4:8, c * 128:(c + 1) * 128], P[:, bp + 1:bp + 2, :],
                  eng=eng, inc=(sn, 1))
            evac_rec[c] = (sn, bump(sn))
        def st_sq(c):
            WAIT("scalar", *xin[c])
            if c >= 2:
                WAIT("scalar", "sV", ssqred[c - 2])
            sact([garb_s, garb_v][c % 2][:, :], xt[c % 8][:, :],
                 ACT.Square, inc=("sS", 1))
            sqr[c] = bump("sS")

        def st_red(c):
            WAIT("vector", *xin[c])
            WAIT("vector", "sS", sqr[c])
            vred(ssq1_sb[:, c:c + 1], [garb_s, garb_v][c % 2][:, :],
                 ALU.add, inc=("sV", 1))
            ssqred[c] = bump("sV")
            vred(s1_sb[:, c:c + 1], xt[c % 8][:, :], ALU.add)
            vts(nm1_sb[:, c:c + 1], s1_sb[:, c:c + 1], -1.0 / D, None, ALU.mult)
            vts(tmpA[:, :], ssq1_sb[:, c:c + 1], 1.0 / D, None, ALU.mult)
            vtt(tmpB[:, :], nm1_sb[:, c:c + 1], nm1_sb[:, c:c + 1], ALU.mult)
            vtt(tmpC[:, :], tmpA[:, :], tmpB[:, :], ALU.subtract)
            vts(w1_sb[:, c:c + 1], tmpC[:, :], 1e-5, None, ALU.add,
                inc=("sV", 1))
            wready[c] = bump("sV")

        def st_lnexp(c):
            WAIT("scalar", "sV", wready[c])
            sact(tmpS[:, :], w1_sb[:, c:c + 1], ACT.Ln)
            sact(rs0_sb[:, c:c + 1], tmpS[:, :], ACT.Exp, scale=-0.5,
                 inc=("sS", 1))
            rs0r[c] = bump("sS")

        def st_newton(c):
            def gstep(fn):
                OP("gpsimd", fn, inc=("sG", 1))
                v = bump("sG")
                WAIT("gpsimd", "sG", v)
                return v

            WAIT("gpsimd", "sS", rs0r[c])
            gstep(lambda e, c0=c: e.tensor_tensor(
                ptA[:, :], rs0_sb[:, c0:c0 + 1], rs0_sb[:, c0:c0 + 1],
                ALU.mult))
            gstep(lambda e, c0=c: e.tensor_tensor(
                ptB[:, :], w1_sb[:, c0:c0 + 1], ptA[:, :], ALU.mult))
            gstep(lambda e: e.tensor_scalar(
                ptC[:, :], ptB[:, :], -0.5, 1.5, ALU.mult, ALU.add))
            gstep(lambda e, c0=c: e.tensor_tensor(
                rs1_sb[:, c0:c0 + 1], rs0_sb[:, c0:c0 + 1], ptC[:, :],
                ALU.mult))
            v = gstep(lambda e, c0=c: e.tensor_tensor(
                nmrs1_sb[:, c0:c0 + 1], nm1_sb[:, c0:c0 + 1],
                rs1_sb[:, c0:c0 + 1], ALU.mult))
            nwd[c] = ("sG", v)

        def st_norm(c):
            WAIT("scalar", *nwd[c])
            sact(xn_tm[:, c:c + 1, :], xt[c % 8][:, :], ACT.Identity,
                 bias=nmrs1_sb[:, c:c + 1], scale=rs1_sb[:, c:c + 1],
                 inc=("sS", 1))
            xn_done[c] = ("sS", bump("sS"))
            emit_tr(c)
            if c >= 1:
                emit_evac(c - 1)

        nwd = {}
        for t in range(NTC + 4):
            if t < NTC and t % 4 == 0:
                for c in range(t, t + 4):
                    w = [xn_done[c - 8]] if c >= 8 else []
                    xin[c] = dqX.issue(xt[c % 8][:, :],
                                       x_d[c * 128:(c + 1) * 128, :], waits=w)
            if t < NTC:
                st_sq(t)
            if 0 <= t - 1 < NTC:
                st_red(t - 1)
            if 0 <= t - 2 < NTC:
                st_lnexp(t - 2)
            if 0 <= t - 3 < NTC:
                st_newton(t - 3)
            if 0 <= t - 4 < NTC:
                st_norm(t - 4)
        emit_evac(NTC - 1)

        if upto <= 1:
            barrier()
            raise _Stop()
        if upto <= 2:
            barrier()
            raise _Stop()

        for ic in range(8):
            gdma("F0_%d" % ic, F_w[:, ic:ic + 1, :], wr("Wq")[:, ic:ic + 1, :],
                 waits=[("sT", tr2[ic])])

        d_fk0 = dqX.issue(fkh[0][:, :, :], wr("fk")[:, 0:4, :],
                          waits=[xn_done[11]])
        d_fk1 = dqX.issue(fkh[1][:, :, :], wr("fk")[:, 4:8, :],
                          waits=[xn_done[15]])
        for h in range(2):
            WAIT("vector", *(d_fk0 if h == 0 else d_fk1))
            vred(qs4_sb[:, :], fkh[h][:, :, :], ALU.max, absval=True)
            vred(qs_sb[:, h:h + 1], qs4_sb[:, :], ALU.max)
        vtt(qs_sb[:, 2:3], qs_sb[:, 0:1], qs_sb[:, 1:2], ALU.max, inc=("sV", 1))
        v_qmax = bump("sV")
        WAIT("tensor", "sV", v_qmax)
        WAIT("tensor", "sG", g_ident)
        WAIT("tensor", evac_rec[14][0], evac_rec[14][1])
        WAIT("tensor", evac_rec[15][0], evac_rec[15][1])
        mm(P[0:1, 2:3, 0:128], qs_sb[:, 2:3], ident_f[:, :], start=True,
           stop=True, inc=("sT", 1))
        t_qt = bump("sT")
        WAIT("vector", "sT", t_qt)
        vcopy(qrow[0:1, :], P[0:1, 2:3, 0:128])
        vred(qs_sb[0:1, 3:4], qrow[0:1, :], ALU.max, inc=("sV", 1))
        v_qrow = bump("sV")
        WAIT("tensor", "sV", v_qrow)
        mm(P[:, 2:3, 0:1], ones_row_f[0:1, :], qs_sb[0:1, 3:4], start=True,
           stop=True, inc=("sT", 1))
        t_qb = bump("sT")
        WAIT("vector", "sT", t_qb)
        vcopy(qs_sb[:, 3:4], P[:, 2:3, 0:1])
        OP("vector", lambda e: e.memset(tmpA[:, :], 7.0))
        vrecip(tmpB[:, :], tmpA[:, :])
        vtt(qs_sb[:, 4:5], qs_sb[:, 3:4], tmpB[:, :], ALU.mult)
        vrecip(qs_sb[:, 5:6], qs_sb[:, 4:5])
        vts(qs_sb[:, 6:7], qs_sb[:, 4:5], 8.0, None, ALU.mult)

        for ic in range(8):
            gdma("E0_%d" % ic, E_w[:, ic:ic + 1, :],
                 wr("Wq2")[:, ic:ic + 1, :], waits=[("sT", tr2[8 + ic])])

        WAIT("tensor", *wv["fa"])
        WAIT("tensor", "sG", g_ident)
        fat_g = {}
        for c in range(8):
            bank = 6 + c // 4
            mm(P[0:16, bank:bank + 1, (c % 4) * 128:(c % 4) * 128 + 128],
               fa_sb[:, c:c + 1, :], ident_b[:, :],
               start=(c % 4 == 0), stop=(c % 4 == 3),
               inc=("sT", 1) if c % 4 == 3 else None)
            if c % 4 == 3:
                fat_g[c // 4] = bump("sT")
        for i in range(2):
            WAIT("vector", "sT", fat_g[i])
            vcopy(faT[0:16, i * 512:(i + 1) * 512], P[0:16, 6 + i:7 + i, :],
                  inc=("sV", 1))
            v_fat = bump("sV")

        gdma("stA0", stageA[:, :], rowr("bq"), waits=[("sV", ssqred[15])])
        gdma("stB0", stageB[:, :], rowr("bq2"))

        barrier()
        if upto <= 3:
            raise _Stop()

        WEIGHTS = [
            ("q", "Wq", "Wq2", "bq", "bq2", Wpq, ("row", bqp_row)),
            ("k", "Wk", "Wk2", "bk", "bk2", Wpk, ("row", bkp_row)),
            ("v", "Wv", "Wv2", "bv", "bv2", Wpv, ("col", bvp_col)),
        ]
        tr_done = {}
        col_done = {}
        bias_done = {}
        prod_done = {}
        prod_evac_done = {}
        for wi, (tag, w1n, w2n, b1n, b2n, Wp_dst, brow) in enumerate(WEIGHTS):
            if wi >= 1:
                gdma("F%da" % wi, F_w[:, 0:4, :], wr(w1n)[:, 0:4, :],
                     waits=[("sT", tr_done[wi - 1])])
                gdma("F%d" % wi, F_w[:, 4:8, :], wr(w1n)[:, 4:8, :],
                     waits=[("sT", tr_done[wi - 1])])
                gdma("E%da" % wi, E_w[:, 0:4, :], wr(w2n)[:, 0:4, :],
                     waits=[("sT", prod_done[wi - 1])])
                gdma("E%d" % wi, E_w[:, 4:8, :], wr(w2n)[:, 4:8, :],
                     waits=[("sT", prod_done[wi - 1])])
                gdma("stA%d" % wi, stageA[:, :], rowr(b1n),
                     waits=[("sT", col_done[wi - 1])])
                gdma("stB%d" % wi, stageB[:, :], rowr(b2n),
                     waits=[("sT", bias_done[wi - 1])])
            fk_ = "F%d" % wi if wi >= 1 else "F0"
            ek_ = "E%d" % wi if wi >= 1 else "E0"
            if wi == 0:
                for ic in range(4, 8):
                    WAIT("tensor", *wv["F0_%d" % ic])
            else:
                WAIT("tensor", *wv[fk_])
                WAIT("tensor", *wv[fk_ + "a"])
            if wi >= 1:
                WAIT("tensor", "sS", prod_evac_done[wi - 1])
            wt_ev = {}
            for kc in range(8):
                for ich in range(2):
                    g = kc * 2 + ich
                    bank = g % 2
                    if g >= 2:
                        WAIT("tensor", "sV", wt_ev[g - 2])
                    for k in range(4):
                        ic = ich * 4 + k
                        mm(P[:, bank:bank + 1, k * 128:(k + 1) * 128],
                           F_w[:, ic:ic + 1, kc * 128:(kc + 1) * 128],
                           ident_b[:, :], start=(k == 0), stop=(k == 3),
                           inc=("sT", 1) if k == 3 else None)
                    t_g = bump("sT")
                    WAIT("vector", "sT", t_g)
                    vcopy(WT_w[:, kc:kc + 1, ich * 512:(ich + 1) * 512],
                          P[:, bank:bank + 1, :], inc=("sV", 1))
                    wt_ev[g] = bump("sV")
            tr_done[wi] = cnt["sT"]
            WAIT("tensor", *wv["stA%d" % wi])
            for kcc in range(8):
                mm(P[:, 2:3, kcc:kcc + 1],
                   stageA[0:1, kcc * 128:(kcc + 1) * 128], ones_col[0:1, 0:1],
                   start=(kcc == 0), stop=(kcc == 7),
                   inc=("sT", 1) if kcc == 7 else None)
            col_done[wi] = bump("sT")
            WAIT("vector", "sT", col_done[wi])
            vcopy(bcol_st[:, 0:8], P[:, 2:3, 0:8], inc=("sV", 1))
            v_col = bump("sV")
            WAIT("tensor", "sV", v_col)
            if wi == 0:
                for ic in range(4, 8):
                    WAIT("tensor", *wv["E0_%d" % ic])
            else:
                WAIT("tensor", *wv[ek_])
                WAIT("tensor", *wv[ek_ + "a"])
            WAIT("tensor", *wv["stB%d" % wi])
            for jh in range(2):
                bank = 4 + jh
                for kcc in range(8):
                    mm(P[0:1, bank:bank + 1, :], bcol_st[:, kcc:kcc + 1],
                       E_w[:, kcc:kcc + 1, jh * 512:(jh + 1) * 512],
                       start=(kcc == 0), stop=False)
                mm(P[0:1, bank:bank + 1, :], ones_col[0:1, 0:1],
                   stageB[0:1, jh * 512:(jh + 1) * 512], start=False, stop=True,
                   inc=("sT", 1) if jh == 1 else None)
            bias_done[wi] = bump("sT")
            WAIT("scalar", "sT", bias_done[wi])
            if brow[0] == "row":
                for jh in range(2):
                    scopy(brow[1][0:1, jh * 512:(jh + 1) * 512],
                          P[0:1, 4 + jh:5 + jh, :],
                          inc=("sS", 1) if jh == 1 else None)
                s_brow = bump("sS")
            else:
                for jh in range(2):
                    scopy(bvp_row[0:1, jh * 512:(jh + 1) * 512],
                          P[0:1, 4 + jh:5 + jh, :],
                          inc=("sS", 1) if jh == 1 else None)
                s_brow = bump("sS")
                WAIT("tensor", "sS", s_brow)
                for kcc in range(8):
                    mm(P[:, 6:7, kcc:kcc + 1],
                       bvp_row[0:1, kcc * 128:(kcc + 1) * 128], ones_f[0:1, 0:1],
                       start=(kcc == 0), stop=(kcc == 7),
                       inc=("sT", 1) if kcc == 7 else None)
                t_bvp = bump("sT")
                WAIT("vector", "sT", t_bvp)
                vcopy(bvp_col[:, 0:8], P[:, 6:7, 0:8], inc=("sV", 1))
                bump("sV")
            WAIT("tensor", "sV", wt_ev[15])
            WAIT("tensor", "sS", cnt["sS"])
            if brow[0] == "col":
                WAIT("tensor", "sV", cnt["sV"])
            pe_v = {}
            for jh in range(2):
                if jh == 1:
                    WAIT("tensor", "sS", pe_v[0])
                for kc in range(8):
                    for ic in range(8):
                        mm(P[:, ic:ic + 1, :],
                           WT_w[:, kc:kc + 1, ic * 128:(ic + 1) * 128],
                           E_w[:, kc:kc + 1, jh * 512:(jh + 1) * 512],
                           start=(kc == 0), stop=(kc == 7),
                           inc=("sT", 1) if (kc == 7 and ic == 7) else None)
                t_pj = bump("sT")
                WAIT("scalar", "sT", t_pj)
                for ic in range(8):
                    scopy(Wp_dst[:, ic:ic + 1, jh * 512:(jh + 1) * 512],
                          P[:, ic:ic + 1, :], inc=("sS", 1) if ic == 7 else None)
                pe_v[jh] = bump("sS")
                if jh == 1:
                    prod_done[wi] = t_pj
                    prod_evac_done[wi] = pe_v[1]

        barrier()
        if upto <= 4:
            raise _Stop()

        q2pre_v = {}
        k2pre_v = {}
        evq = {}
        evk = {}
        q2cd = {}
        k2cd = {}
        qm_g = {}
        vq_done = {}
        S_done = {}
        GE = ALU.is_ge
        q_rinv = qs_sb[:, 5:6]
        q_scale = qs_sb[:, 4:5]
        q_s8 = qs_sb[:, 6:7]
        d_q8 = {}
        lq_done = {}
        for qi in range(2):
            d_q8[qi] = dqX.issue(qsrc[qi][:, :, :], wr("fk")[:, qi:qi + 1, :])

        def gtt2(out, a, b, op, inc=None):
            OP("gpsimd",
               lambda e, o=out, x=a, y=b, p=op: e.tensor_tensor(o, x, y, p),
               inc=inc)

        def quant_part(qi, half):
            wdst = Weff[:, qi:qi + 1, :]
            S = qsrc[qi % 2][:, :, :]
            if half == 0:
                WAIT("vector", *d_q8[qi])
                vts(qA[:, :, :], S, q_rinv, 8.5, ALU.mult, ALU.add)
                vts(qB[:, :, :], qA[:, :, :], 8.0, None, GE)
                vstt(qC[:, :, :], qB[:, :, :], -8.0, qA[:, :, :],
                     ALU.mult, ALU.add)
                vts(qB[:, :, :], qC[:, :, :], 4.0, None, GE)
                vstt(S, qB[:, :, :], -4.0, qC[:, :, :], ALU.mult, ALU.add)
                vts(qB[:, :, :], S, 2.0, None, GE)
            else:
                vstt(qC[:, :, :], qB[:, :, :], -2.0, S, ALU.mult, ALU.add)
                vts(qB[:, :, :], qC[:, :, :], 1.0, None, GE)
                vstt(S, qB[:, :, :], -1.0, qC[:, :, :], ALU.mult, ALU.add)
                vtt(qC[:, :, :], qA[:, :, :], S, ALU.subtract)
                vts(wdst, qC[:, :, :], q_scale, q_s8, ALU.mult,
                    ALU.subtract, inc=("sV", 1))
                lq_done[qi] = ("sV", bump("sV"))

        for c in range(NTC):
            if c >= 1:
                WAIT("tensor", "sS", evk[c - 1])
            for kc in range(8):
                lh = xnT[:, kc:kc + 1, c * 128:(c + 1) * 128]
                mm(P[:, 0:1, :], lh, Wpq[:, kc:kc + 1, 0:512],
                   start=(kc == 0), stop=False)
                mm(P[:, 1:2, :], lh, Wpq[:, kc:kc + 1, 512:1024],
                   start=(kc == 0), stop=False)
                mm(P[:, 2:3, :], lh, Wpk[:, kc:kc + 1, 0:512],
                   start=(kc == 0), stop=False)
                mm(P[:, 3:4, :], lh, Wpk[:, kc:kc + 1, 512:1024],
                   start=(kc == 0), stop=False)
            mm(P[:, 0:1, :], ones_col[0:1, :], bqp_row[0:1, 0:512],
               start=False, stop=True)
            mm(P[:, 1:2, :], ones_col[0:1, :], bqp_row[0:1, 512:1024],
               start=False, stop=True, inc=("sT", 1))
            q2pre_v[c] = bump("sT")
            mm(P[:, 2:3, :], ones_col[0:1, :], bkp_row[0:1, 0:512],
               start=False, stop=True)
            mm(P[:, 3:4, :], ones_col[0:1, :], bkp_row[0:1, 512:1024],
               start=False, stop=True, inc=("sT", 1))
            k2pre_v[c] = bump("sT")
            if c >= 1:
                cp = c - 1
                WAIT("tensor", "sG", q2cd[cp])
                WAIT("tensor", "sG", k2cd[cp])
                for p in range(8):
                    bank = 4 + p // 4
                    mm(P[:, bank:bank + 1, (p % 4) * 128:(p % 4) * 128 + 128],
                       k2c[cp % 2][:, p * 128:(p + 1) * 128],
                       q2c[cp % 2][:, p * 128:(p + 1) * 128],
                       start=(cp == 0 and p % 4 == 0), stop=False,
                       inc=("sT", 1) if p == 7 else None)
                S_done[cp] = bump("sT")
            WAIT("scalar", "sT", q2pre_v[c])
            scopy(qk_pre[:, 0:2, :], P[:, 0:2, :], inc=("sS", 1))
            evq[c] = bump("sS")
            WAIT("scalar", "sT", k2pre_v[c])
            scopy(qk_pre[:, 2:4, :], P[:, 2:4, :], inc=("sS", 1))
            evk[c] = bump("sS")
            if c >= 1:
                WAIT("scalar", "sG", k2cd[c - 1])
            sact(m_scr[:, :, :], qk_pre[:, 0:2, :], ACT.Relu, scale=-1.0)
            sact(m_k[:, :, :], qk_pre[:, 2:4, :], ACT.Relu, scale=-1.0)
            sact(r_q2[:, :, :], qk_pre[:, 0:2, :], ACT.Relu)
            sact(r_k2[:, :, :], qk_pre[:, 2:4, :], ACT.Relu)
            sact(e_scr[:, :, :], m_scr[:, :, :], ACT.Exp, scale=-1.0,
                 inc=("sS", 1))
            s_eq = bump("sS")
            sact(e_k[:, :, :], m_k[:, :, :], ACT.Exp, scale=-1.0,
                 inc=("sS", 1))
            s_ek = bump("sS")
            if c >= 2:
                WAIT("gpsimd", "sT", S_done[c - 2])
            WAIT("gpsimd", "sS", s_eq)
            gtt2(q2c[c % 2][:, :], e_scr[:, :, :], r_q2[:, :, :], ALU.add,
                 inc=("sG", 1))
            q2cd[c] = bump("sG")
            WAIT("gpsimd", "sS", s_ek)
            gtt2(k2c[c % 2][:, :], e_k[:, :, :], r_k2[:, :, :], ALU.add,
                 inc=("sG", 1))
            k2cd[c] = bump("sG")
            if c >= 1:
                WAIT("gpsimd", "sV", vq_done[c - 1])
            WAIT("gpsimd", "sG", k2cd[c])
            gtt2(m_scrR[:, :, :], q2c[c % 2][:, :], k2c[c % 2][:, :],
                 ALU.mult, inc=("sG", 1))
            qm_g[c] = bump("sG")
            WAIT("vector", "sG", qm_g[c])
            vred(qks_sb[:, c * 16:(c + 1) * 16], m_scrR[:, :, :], ALU.add,
                 inc=("sV", 1))
            vq_done[c] = bump("sV")
            quant_part(c // 2, c % 2)
            if c % 2 == 1 and c < 12:
                qn = c // 2 + 2
                d_q8[qn] = dqX.issue(qsrc[qn % 2][:, :, :],
                                     wr("fk")[:, qn:qn + 1, :],
                                     waits=[lq_done[c // 2]])
        cp = NTC - 1
        WAIT("tensor", "sG", q2cd[cp])
        WAIT("tensor", "sG", k2cd[cp])
        for p in range(8):
            bank = 4 + p // 4
            mm(P[:, bank:bank + 1, (p % 4) * 128:(p % 4) * 128 + 128],
               k2c[cp % 2][:, p * 128:(p + 1) * 128],
               q2c[cp % 2][:, p * 128:(p + 1) * 128],
               start=False, stop=(p % 4 == 3),
               inc=("sT", 1) if p == 7 else None)
        S_done[cp] = bump("sT")
        WAIT("tensor", "sV", v_fat)
        WAIT("tensor", *wv["fb"])
        WAIT("tensor", *lq_done[7])
        wfev = {}
        for ic in range(8):
            for jf in range(2):
                g = ic * 2 + jf
                bank = 6 + g % 2
                if g >= 2:
                    WAIT("tensor", "sS", wfev[g - 2])
                mm(P[:, bank:bank + 1, :], faT[0:16, ic * 128:(ic + 1) * 128],
                   fb_sb[0:16, jf * 512:(jf + 1) * 512], start=True, stop=False)
                mm(P[:, bank:bank + 1, :], ident_b[:, :],
                   Weff[:, ic:ic + 1, jf * 512:(jf + 1) * 512],
                   start=False, stop=True, inc=("sT", 1))
                t_wf = bump("sT")
                WAIT("scalar", "sT", t_wf)
                scopy(Weff[:, ic:ic + 1, jf * 512:(jf + 1) * 512],
                      P[:, bank:bank + 1, :], inc=("sS", 1))
                wfev[g] = bump("sS")
        vts(rqA_sb[:, :], qks_sb[:, :], 1e-6, None, ALU.add)
        vrecip(rqB_sb[:, :], rqA_sb[:, :])
        OP("vector", lambda e: e.engine_nop(), inc=("sV", 1))
        v_rq = bump("sV")

        barrier()
        if upto <= 5:
            raise _Stop()

        rqt_ev = {}
        WAIT("tensor", "sV", v_rq)
        for g in range(4):
            bank = 6 + g % 2
            if g >= 2:
                WAIT("tensor", "sV", rqt_ev[g - 2])
            for j in range(4):
                tc = g * 4 + j
                mm(P[0:16, bank:bank + 1, j * 128:(j + 1) * 128],
                   rqB_sb[:, tc * 16:(tc + 1) * 16], ident_f[:, :],
                   start=(j == 0), stop=(j == 3), tr=True,
                   inc=("sT", 1) if j == 3 else None)
            t_rg = bump("sT")
            WAIT("vector", "sT", t_rg)
            vcopy(rqT[0:16, g * 512:(g + 1) * 512], P[0:16, bank:bank + 1, :],
                  inc=("sV", 1))
            rqt_ev[g] = bump("sV")
        v2ev = {}
        for mc in range(8):
            for u in range(2):
                unit = mc * 2 + u
                bs = (unit % 2) * 2
                if unit >= 2:
                    WAIT("tensor", "sS", v2ev[unit - 2])
                for kc in range(8):
                    for t2 in range(2):
                        th = u * 2 + t2
                        mm(P[:, bs + t2:bs + t2 + 1, :],
                           Wpv[:, kc:kc + 1, mc * 128:(mc + 1) * 128],
                           xnT[:, kc:kc + 1, th * 512:(th + 1) * 512],
                           start=(kc == 0), stop=(kc == 7),
                           inc=("sT", 1) if (kc == 7 and t2 == 1) else None)
                t_u = bump("sT")
                WAIT("scalar", "sT", t_u)
                for t2 in range(2):
                    th = u * 2 + t2
                    sact(v2T[:, mc:mc + 1, th * 512:(th + 1) * 512],
                         P[:, bs + t2:bs + t2 + 1, :], ACT.Identity,
                         bias=bvp_col[:, mc:mc + 1],
                         inc=("sS", 1) if t2 == 1 else None)
                v2ev[unit] = bump("sS")

        barrier()
        if upto <= 6:
            raise _Stop()

        OP("vector", lambda e: e.memset(Sbd[:, :, :], 0.0))
        for p in range(8):
            bank = 4 + p // 4
            off = (p % 4) * 128
            vcopy(Sbd[0:64, p:p + 1, 0:64], P[0:64, bank:bank + 1, off:off + 64])
            vcopy(Sbd[64:128, p:p + 1, 64:128],
                  P[64:128, bank:bank + 1, off + 64:off + 128],
                  inc=("sV", 1) if p == 7 else None)
        v_sbd = bump("sV")
        WAIT("vector", "sV", rqt_ev[3])
        vcopy(rqTb[0:16, :], rqT[0:16, :], inc=("sV", 1))
        v_rqtb = bump("sV")
        norm = {}
        rqev = {}
        for p in range(8):
            if p == 0:
                WAIT("tensor", "sV", v_sbd)
                WAIT("tensor", "sV", v_rqtb)
            if p >= 1:
                WAIT("tensor", "sS", rqev[p - 1])
            for th in range(4):
                mm(P[:, 4 + th:5 + th, :], selp[0:16, p:p + 1, :, :],
                   rqTb[0:16, th * 512:(th + 1) * 512], start=True, stop=True,
                   inc=("sT", 1) if th == 3 else None)
            t_rb = bump("sT")
            WAIT("scalar", "sT", t_rb)
            if p >= 1:
                WAIT("scalar", "sV", norm[p - 1])
            for th in range(4):
                scopy(rqbc_sb[:, th * 512:(th + 1) * 512], P[:, 4 + th:5 + th, :],
                      inc=("sS", 1) if th == 3 else None)
            rqev[p] = bump("sS")
            if p >= 1:
                WAIT("tensor", "sV", norm[p - 1])
            for th in range(4):
                mm(P[:, th:th + 1, :], Sbd[:, p:p + 1, :],
                   v2T[:, p:p + 1, th * 512:(th + 1) * 512], start=True, stop=True,
                   inc=("sT", 1) if th == 3 else None)
            t_o = bump("sT")
            WAIT("vector", "sT", t_o)
            WAIT("vector", "sS", rqev[p])
            for th in range(4):
                vtt(attnT[:, p:p + 1, th * 512:(th + 1) * 512],
                    P[:, th:th + 1, :],
                    rqbc_sb[:, th * 512:(th + 1) * 512], ALU.mult,
                    inc=("sV", 1) if th == 3 else None)
            norm[p] = bump("sV")

        barrier()
        if upto <= 7:
            raise _Stop()

        WAIT("tensor", *wv["gamma"])
        for jh in range(2):
            mm(P[:, 6 + jh:7 + jh, :], ones_row_f[0:1, :],
               gamma_row[0:1, jh * 512:(jh + 1) * 512], start=True, stop=True,
               inc=("sT", 1) if jh == 1 else None)
        t_gt = bump("sT")
        WAIT("vector", "sT", t_gt)
        vcopy(gam_t[:, 0:512], P[:, 6:7, :])
        vcopy(gam_t[:, 512:1024], P[:, 7:8, :], inc=("sV", 1))
        v_gt = bump("sV")

        xrin = {}
        for i in range(4):
            xrin[i] = dqR.issue(xr[i][:, :, :], x_d[i * 128:(i + 1) * 128, :])

        ymm = {}
        fmm = {}
        xnr = {}
        yused = {}
        o1used = {}
        outt_v = {}
        s2r = {}
        w2r = {}
        r02 = {}
        nrm2 = {}
        t2r = {}
        gdone = {}
        odma = {}
        for s in range(NTC + 2):
            c1 = s
            c2 = s - 1
            c3 = s - 2
            if c1 < NTC:
                c = c1
                b0 = (c % 2) * 2
                if c >= 2:
                    WAIT("tensor", "sV", yused[c - 2])
                for jh in range(2):
                    for kc in range(8):
                        mm(P[:, b0 + jh:b0 + jh + 1, :],
                           attnT[:, kc:kc + 1, c * 128:(c + 1) * 128],
                           Wo_sb[:, kc:kc + 1, jh * 512:(jh + 1) * 512],
                           start=(kc == 0), stop=False)
                    mm(P[:, b0 + jh:b0 + jh + 1, :], ones_col[0:1, :],
                       bo_row[0:1, jh * 512:(jh + 1) * 512], start=False,
                       stop=True, inc=("sT", 1) if jh == 1 else None)
                ymm[c] = bump("sT")
            if 0 <= c2 < NTC:
                c = c2
                b0 = (c % 2) * 2
                if c >= 2:
                    WAIT("tensor", "sV", o1used[c - 2])
                if c == 1:
                    WAIT("tensor", "sV", v_gt)
                for jh in range(2):
                    for kc in range(8):
                        mm(P[:, 4 + b0 + jh:5 + b0 + jh, :],
                           xnT[:, kc:kc + 1, c * 128:(c + 1) * 128],
                           Weff[:, kc:kc + 1, jh * 512:(jh + 1) * 512],
                           start=(kc == 0), stop=False)
                    mm(P[:, 4 + b0 + jh:5 + b0 + jh, :], ones_col[0:1, :],
                       cfb_row[0:1, jh * 512:(jh + 1) * 512], start=False,
                       stop=True, inc=("sT", 1) if jh == 1 else None)
                fmm[c] = bump("sT")
            if c1 < NTC:
                c = c1
                WAIT("scalar", *xrin[c])
                if c >= 2:
                    WAIT("scalar", "sV", yused[c - 2])
                sact(xn_rec[c % 2][:, :, :], xr[c % 4][:, :, :], ACT.Identity,
                     bias=nmrs1_sb[:, c:c + 1], scale=rs1_sb[:, c:c + 1],
                     inc=("sS", 1))
                xnr[c] = bump("sS")
            if 0 <= c2 < NTC:
                c = c2
                WAIT("scalar", "sV", yused[c])
                sact(sq_scr[:, :, :], y_scr[c % 3][:, :, :], ACT.Square,
                     inc=("sS", 1))
                s2r[c] = bump("sS")
                WAIT("vector", "sS", s2r[c])
                vred(qs4_sb[:, 0:2], y_scr[c % 3][:, :, :], ALU.add)
                vred(s2_sb[:, c:c + 1], qs4_sb[:, 0:2], ALU.add)
                vred(qs4_sb[:, 2:4], sq_scr[:, :, :], ALU.add)
                vred(ssq2_sb[:, c:c + 1], qs4_sb[:, 2:4], ALU.add)
                vts(nm2_sb[:, c:c + 1], s2_sb[:, c:c + 1], -1.0 / D, None,
                    ALU.mult)
                vts(tmpA[:, :], ssq2_sb[:, c:c + 1], 1.0 / D, None, ALU.mult)
                vtt(tmpB[:, :], nm2_sb[:, c:c + 1], nm2_sb[:, c:c + 1],
                    ALU.mult)
                vtt(tmpC[:, :], tmpA[:, :], tmpB[:, :], ALU.subtract)
                vts(w2_sb[:, c:c + 1], tmpC[:, :], 1e-3, None, ALU.add,
                    inc=("sV", 1))
                w2r[c] = bump("sV")
                WAIT("scalar", "sV", w2r[c])
                sact(tmpS[:, :], w2_sb[:, c:c + 1], ACT.Ln)
                sact(rs02_sb[:, c:c + 1], tmpS[:, :], ACT.Exp, scale=-0.5,
                     inc=("sS", 1))
                r02[c] = bump("sS")
                WAIT("vector", "sS", r02[c])
                vtt(tmpA[:, :], rs02_sb[:, c:c + 1], rs02_sb[:, c:c + 1],
                    ALU.mult)
                vtt(tmpB[:, :], w2_sb[:, c:c + 1], tmpA[:, :], ALU.mult)
                vts(tmpC[:, :], tmpB[:, :], -0.5, 1.5, ALU.mult, ALU.add)
                vtt(rs2_sb[:, c:c + 1], rs02_sb[:, c:c + 1], tmpC[:, :],
                    ALU.mult)
                vtt(nmrs2_sb[:, c:c + 1], nm2_sb[:, c:c + 1],
                    rs2_sb[:, c:c + 1], ALU.mult, inc=("sV", 1))
                nrm2[c] = bump("sV")
            if 0 <= c3 < NTC:
                c = c3
                WAIT("scalar", "sV", nrm2[c])
                if c >= 1:
                    WAIT("scalar", "sG", gdone[c - 1])
                sact(t2_scr[:, :, :], y_scr[c % 3][:, :, :], ACT.Identity,
                     scale=rs2_sb[:, c:c + 1], bias=nmrs2_sb[:, c:c + 1],
                     inc=("sS", 1))
                t2r[c] = bump("sS")
            if c1 < NTC:
                c = c1
                WAIT("vector", "sT", ymm[c])
                WAIT("vector", "sS", xnr[c])
                if c >= 3:
                    WAIT("vector", "sS", t2r[c - 3])
                vtt(y_scr[c % 3][:, :, :], P[:, (c % 2) * 2:(c % 2) * 2 + 2, :],
                    xn_rec[c % 2][:, :, :], ALU.add, inc=("sV", 1))
                yused[c] = bump("sV")
            if 0 <= c3 < NTC:
                c = c3
                if c == 0:
                    WAIT("gpsimd", "sV", v_gt)
                WAIT("gpsimd", "sS", t2r[c])
                if c >= 1:
                    WAIT("gpsimd", "sV", o1used[c - 1])
                OP("gpsimd", lambda e, c0=c: e.tensor_tensor(
                    g_scr[:, :, :], t2_scr[:, :, :], gam_t[:, :], ALU.mult),
                    inc=("sG", 1))
                gdone[c] = bump("sG")
                WAIT("vector", "sG", gdone[c])
                WAIT("vector", "sT", fmm[c])
                if c >= 1:
                    WAIT("vector", "sG", outt_v[c - 1])
                vtt(o1_scr[:, :, :], P[:, 4 + (c % 2) * 2:6 + (c % 2) * 2, :],
                    g_scr[:, :, :], ALU.add, inc=("sV", 1))
                o1used[c] = bump("sV")
                WAIT("gpsimd", "sV", o1used[c])
                if c >= 2:
                    WAIT("gpsimd", *odma[c - 2])
                OP("gpsimd", lambda e, c0=c: e.tensor_tensor(
                    ot[c0 % 2][:, :, :], o1_scr[:, :, :],
                    xr[c0 % 4][:, :, :], ALU.add), inc=("sG", 1))
                outt_v[c] = bump("sG")
                odma[c] = dqO.issue(out_d[c * 128:(c + 1) * 128, :],
                                    ot[c % 2][:, :, :],
                                    waits=[("sG", outt_v[c])])
                if c + 4 < NTC:
                    xrin[c + 4] = dqX.issue(
                        xr[(c + 4) % 4][:, :, :],
                        x_d[(c + 4) * 128:(c + 5) * 128, :],
                        waits=[("sG", outt_v[c])])

    try:
        _phases()
    except _Stop:
        pass

    DBG = {"s1": s1_sb, "ssq1": ssq1_sb, "w1": w1_sb, "rs1": rs1_sb,
           "rs0": rs0_sb, "nm1": nm1_sb, "nmrs1": nmrs1_sb,
           "xn_tm": xn_tm, "xnT": xnT, "Weff": Weff, "Wpv": Wpv,
           "Wpq": Wpq, "Wpk": Wpk, "qs": qs_sb, "qks": qks_sb,
           "rqB": rqB_sb, "Sbd": Sbd, "v2T": v2T, "attnT": attnT,
           "bqp_row": bqp_row, "bkp_row": bkp_row, "bvp_col": bvp_col,
           "rqT": rqT, "s2": s2_sb, "ssq2": ssq2_sb, "rs2": rs2_sb,
           "ident_b": ident_b, "selp": selp, "faT": faT, "cfb_row": cfb_row,
           "gam_t": gam_t, "WT_w": WT_w, "F_w": F_w, "E_w": E_w}
    for name in dump:
        h = DBG[name]
        dbg_d = nc.dram_tensor("dbg_" + name, list(h.shape), h.dtype,
                               kind="ExternalOutput")
        sl = tuple(slice(None) for _ in h.shape)
        dqX.issue(dbg_d[sl], h[sl])

    dqO.drain()
    dqW.drain()
    dqR.drain()
    dqX.drain()
    barrier()

    global _LAST_CNT
    _LAST_CNT = dict(cnt)
    with nc.Block() as blk:
        @blk.tensor
        def _(e):
            for f in plan["tensor"]:
                f(e)

        @blk.vector
        def _(e):
            for f in plan["vector"]:
                f(e)

        @blk.scalar
        def _(e):
            for f in plan["scalar"]:
                f(e)

        @blk.sync
        def _(e):
            for f in plan["sync"]:
                f(e)

        @blk.gpsimd
        def _(e):
            for f in plan["gpsimd"]:
                f(e)

    es.close()
    return nc


_NC_CACHE = None


def kernel(**inputs):
    global _NC_CACHE

    def g(name):
        return np.ascontiguousarray(np.asarray(inputs[name], dtype=np.float32))

    x = g("L0_attn_x")
    common = {
        "Wq": g("L0_attn_Wq"), "bq": g("L0_attn_bq"),
        "Wk": g("L0_attn_Wk"), "bk": g("L0_attn_bk"),
        "Wv": g("L0_attn_Wv"), "bv": g("L0_attn_bv"),
        "Wq2": g("L0_attn_Wq2"), "bq2": g("L0_attn_bq2"),
        "Wk2": g("L0_attn_Wk2"), "bk2": g("L0_attn_bk2"),
        "Wv2": g("L0_attn_Wv2"), "bv2": g("L0_attn_bv2"),
        "Wo": g("L0_attn_Wo"), "bo": g("L0_attn_bo"),
        "gamma": g("L0_attn_gamma"), "beta": g("L0_attn_beta"),
        "fk": g("L0_ffn_kernel"), "fbias": g("L0_ffn_bias"),
        "fa": g("L0_ffn_a"), "fb": g("L0_ffn_b"),
    }
    if _NC_CACHE is None:
        _NC_CACHE = _build()
    nc = _NC_CACHE
    in_maps = [dict(common, x=np.ascontiguousarray(x[b])) for b in range(8)]
    res = run_bass_kernel_spmd(nc, in_maps, core_ids=list(range(8)))
    out = np.stack([res.results[b]["out"] for b in range(8)], axis=0)
    return out.astype(np.float32)
